# version 21
# baseline (speedup 1.0000x reference)
import numpy as np
from contextlib import ExitStack

import concourse.bass as bass
import concourse.mybir as mybir
from concourse.bass_utils import run_bass_kernel_spmd

F32 = mybir.dt.float32
BF16 = mybir.dt.bfloat16
I32 = mybir.dt.int32
AF = mybir.ActivationFunctionType
ALU = mybir.AluOpType
AX = mybir.AxisListType

NSEQ = 2
T = 2048
D = 1024
H = 16
HID = 3072
NT = T // 128
EPS = 1e-5
QS = 1.0 / 127.0

ENGINES = ("sync", "gpsimd", "vector", "scalar", "tensor")


class Prog:
    def __init__(self, nc, sems):
        self.nc = nc
        self.sems = sems
        self.C = {k: 0 for k in sems}
        self.epoch = 0
        self.waited = {}


class Phase:
    def __init__(self, prog, name):
        self.p = prog
        self.name = name
        self.q = {e: [] for e in ENGINES}
        self._ld = {}
        self._st = {}
        self._sd = {}

    def ld(self, chain):
        if chain not in self._ld:
            assert len(self._ld) < 16, "out of ld lanes"
            self._ld[chain] = f"ld{len(self._ld)}"
        return self._ld[chain]

    def st(self, chain):
        if chain not in self._st:
            assert len(self._st) < 8, "out of st lanes"
            self._st[chain] = f"st{len(self._st)}"
        return self._st[chain]

    def sd(self, chain):
        if chain not in self._sd:
            assert len(self._sd) < 4, "out of sl lanes"
            self._sd[chain] = f"sl{len(self._sd)}"
        return self._sd[chain]

    def op(self, eng, fn, deps=(), inc=None, n=1, pre_nops=0, pre_drain=False,
           fdeps=()):
        p = self.p
        waits = {}
        for t in deps:
            if t is None:
                continue
            sem, val, ep, src = t
            if ep < p.epoch:
                continue
            if val > waits.get(sem, -1):
                waits[sem] = val
        wl = []
        for t in fdeps:
            if t is None:
                continue
            sem, val, ep, src = t
            if ep < p.epoch:
                continue
            wl.append((sem, val))
            if val > p.waited.get((eng, sem), -1):
                p.waited[(eng, sem)] = val
            waits.pop(sem, None) if waits.get(sem, -1) <= val else None
        for sem, val in waits.items():
            key = (eng, sem)
            if p.waited.get(key, -1) >= val:
                continue
            p.waited[key] = val
            wl.append((sem, val))
        tok = None
        if inc is not None:
            p.C[inc] += n
            tok = (inc, p.C[inc], p.epoch, eng)
        self.q[eng].append((fn, wl, inc, n, pre_nops, pre_drain))
        return tok

    def run(self):
        p = self.p
        nc = p.nc
        with nc.Block(self.name) as blk:
            for eng_name in ENGINES:
                ops = self.q[eng_name]
                if not ops:
                    continue

                def body(e, ops=ops, eng_name=eng_name):
                    for fn, wl, inc, n, pre_nops, pre_drain in ops:
                        for sem, val in wl:
                            e.wait_ge(p.sems[sem], val)
                        if pre_drain:
                            e.drain()
                        inst = fn(e)
                        if inc is not None:
                            inst.then_inc(p.sems[inc], n)

                getattr(blk, eng_name)(body)
        p.epoch += 1


def build_program(debug=False, stages=("prep", "a1", "a2", "norm", "b", "c")):
    nc = bass.Bass("TRN2", target_bir_lowering=False)

    x_d = nc.dram_tensor("L0_attn_x", [NSEQ, T, D], F32, kind="ExternalInput")
    g_d = nc.dram_tensor("L0_attn_ln_g", [D], F32, kind="ExternalInput")
    b_d = nc.dram_tensor("L0_attn_ln_b", [D], F32, kind="ExternalInput")
    Wq_d = nc.dram_tensor("L0_attn_Wq", [D, D], F32, kind="ExternalInput")
    bq_d = nc.dram_tensor("L0_attn_bq", [D], F32, kind="ExternalInput")
    Wk_d = nc.dram_tensor("L0_attn_Wk", [D, D], F32, kind="ExternalInput")
    bk_d = nc.dram_tensor("L0_attn_bk", [D], F32, kind="ExternalInput")
    Wv_d = nc.dram_tensor("L0_attn_Wv", [D, D], F32, kind="ExternalInput")
    bv_d = nc.dram_tensor("L0_attn_bv", [D], F32, kind="ExternalInput")
    wg_d = nc.dram_tensor("L0_ffn_w_gate", [HID, D], I32, kind="ExternalInput")
    wu_d = nc.dram_tensor("L0_ffn_w_up", [HID, D], I32, kind="ExternalInput")
    wd_d = nc.dram_tensor("L0_ffn_w_down", [D, HID], I32, kind="ExternalInput")
    out_d = nc.dram_tensor("out", [NSEQ, T, D], F32, kind="ExternalOutput")

    sk = "ExternalOutput" if debug else "Internal"
    wg16 = nc.dram_tensor("wg16", [HID, D], BF16, kind="Internal")
    wu16 = nc.dram_tensor("wu16", [HID, D], BF16, kind="Internal")
    wd16 = nc.dram_tensor("wd16", [D, HID], BF16, kind="Internal")
    wgT = nc.dram_tensor("wgT", [12 * 8 * 128 * 256], BF16, kind="Internal")
    wuT = nc.dram_tensor("wuT", [12 * 8 * 128 * 256], BF16, kind="Internal")
    wdT = nc.dram_tensor("wdT", [HID, D], BF16, kind="Internal")
    x1s = nc.dram_tensor("x1s", [T, D], F32, kind=sk)
    dbg_sm = nc.dram_tensor("dbg_sm", [128, 64], F32, kind=sk)
    x2s = nc.dram_tensor("x2s", [T, D], F32, kind=sk)

    es = ExitStack()
    sems = {}
    for s in (["mm", "ev", "sc", "gp"] + [f"ld{i}" for i in range(16)]
              + [f"st{i}" for i in range(8)] + [f"sl{i}" for i in range(4)]):
        sems[s] = es.enter_context(nc.semaphore(s))
    P = Prog(nc, sems)

    SB = lambda name, shape, dt: es.enter_context(nc.sbuf_tensor(name, shape, dt))
    ident = SB("ident", [128, 128], BF16)
    id16 = SB("id16", [16, 16], F32)
    ones_r = SB("ones_r", [1, 512], BF16)
    selp = SB("selp", [128, 128], BF16)
    gch = SB("gch", [128, 8], F32)
    bch = SB("bch", [128, 8], F32)
    bchb = SB("bchb", [128, 8], BF16)
    cb = SB("cb", [128, 16], F32)
    cb8 = SB("cb8", [128, 8], F32)
    crow = SB("crow", [1, 16], BF16)
    crow2 = SB("crow2", [1, 8], BF16)
    n16 = SB("n16", [16, 8], F32)
    sm_s = SB("sm_s", [128, 16], F32)
    sm_q = SB("sm_q", [128, 16], F32)
    sm_m = SB("sm_m", [128, 16], F32)
    sm_r = SB("sm_r", [128, 16], F32)
    sm2_s = SB("sm2_s", [128, 16], F32)
    sm2_q = SB("sm2_q", [128, 16], F32)
    sm2_m = SB("sm2_m", [128, 16], F32)
    sm2_r = SB("sm2_r", [128, 16], F32)
    sm_t1 = SB("sm_t1", [128, 16], F32)
    sm_t2 = SB("sm_t2", [128, 16], F32)
    eps_c = SB("eps_c", [128, 1], F32)
    ones_f = SB("ones_f", [1, 1], F32)

    PS0 = es.enter_context(nc.psum_tensor("PS0", [128, 2048], F32))
    PS1 = es.enter_context(nc.psum_tensor("PS1", [128, 2048], F32))

    def flush(ph):
        for lane in list(ph._ld.values()):
            ph.op("sync", lambda e, l=lane: e.wait_ge(sems[l], P.C[l]))
        for lane in list(ph._st.values()):
            ph.op("gpsimd", lambda e, l=lane: e.wait_ge(sems[l], P.C[l]))
        for lane in list(ph._sd.values()):
            ph.op("scalar", lambda e, l=lane: e.wait_ge(sems[l], P.C[l]))

    def stats_chain(ph, b, t_red, t_sq, prev_done, bufs=None):
        ss, qq, mm_, rr = bufs if bufs else (sm_s, sm_q, sm_m, sm_r)
        c0, c1 = b * 4, b * 4 + 4
        deps = [t_red[b * 4 + i] for i in range(4)] if t_red else []
        t_m = ph.op("scalar", lambda e: e.mul(mm_[:, c0:c1], ss[:, c0:c1], 1.0 / D),
                    deps=deps + ([prev_done] if t_red is None else []), inc="sc")
        t_m2 = ph.op("scalar", lambda e: e.activation(
            sm_t1[:, c0:c1], mm_[:, c0:c1], AF.Square),
            deps=[prev_done, t_m], inc="sc")
        t_var = ph.op("vector", lambda e: e.scalar_tensor_tensor(
            out=sm_t2[:, c0:c1], in0=qq[:, c0:c1], scalar=1.0 / D,
            in1=sm_t1[:, c0:c1], op0=ALU.mult, op1=ALU.subtract),
            deps=[t_m2] + ([t_sq[b * 4 + i] for i in range(4)] if t_sq else []),
            inc="ev")
        t_sd = ph.op("scalar", lambda e: e.activation(
            sm_t1[:, c0:c1], sm_t2[:, c0:c1], AF.Sqrt, bias=eps_c[:, 0:1]),
            deps=[t_var], inc="sc")
        return ph.op("vector", lambda e: e.reciprocal(
            rr[:, c0:c1], sm_t1[:, c0:c1]), deps=[t_sd], inc="ev")

    def prep():
        ph = Phase(P, "prep_const")

        def mset(buf, val):
            return ph.op("gpsimd", lambda e, buf=buf, val=val: e.memset(buf, val),
                         inc="gp")

        def asel(buf, dep, cmp, fill, base, pat, cm):
            return ph.op("gpsimd", lambda e, buf=buf: e.affine_select(
                out=buf, in_=buf, compare_op=cmp, fill=fill, base=base,
                pattern=pat, channel_multiplier=cm), deps=[dep], inc="gp")

        t0 = mset(ident[:], 0.0)
        asel(ident[:], t0, ALU.not_equal, 1.0, 0, [[-1, 128]], 1)
        t0 = mset(id16[:], 0.0)
        asel(id16[:], t0, ALU.not_equal, 1.0, 0, [[-1, 16]], 1)
        mset(ones_r[:], 1.0)
        mset(eps_c[:], EPS)
        t_of = mset(ones_f[:], 1.0)
        for dc in range(8):
            sl_ = selp[:, dc * 16:(dc + 1) * 16]
            t0 = mset(sl_, 1.0)
            t1 = asel(sl_, t0, ALU.is_ge, 0.0, 128 * dc, [[-64, 16]], 1)
            asel(sl_, t1, ALU.is_ge, 0.0, 63 - 128 * dc, [[64, 16]], -1)
        ph.run()

        ph = Phase(P, "prep_gb")
        with nc.sbuf_tensor("pgb_row", [1, 2048], F32) as grow:
            t_g = ph.op("sync", lambda e: e.dma_start(
                grow[0:1, 0:1024], g_d.ap().rearrange("(a d) -> a d", a=1)),
                inc=ph.ld("ga"), n=16)
            t_b = ph.op("sync", lambda e: e.dma_start(
                grow[0:1, 1024:2048], b_d.ap().rearrange("(a d) -> a d", a=1)),
                inc=ph.ld("gb"), n=16)

            def gmm(e):
                for dc in range(16):
                    inst = e.matmul(PS0[:, dc:dc + 1],
                                    grow[0:1, dc * 128:(dc + 1) * 128],
                                    ones_f[0:1, 0:1], start=True, stop=True)
                return inst

            t_mm = ph.op("tensor", gmm, deps=[t_g, t_b, t_of], inc="mm")
            ph.op("vector", lambda e: e.tensor_copy(gch[:], PS0[:, 0:8]),
                  deps=[t_mm], inc="ev")
            t_bc2 = ph.op("vector", lambda e: e.tensor_copy(bch[:], PS0[:, 8:16]),
                          deps=[t_mm], inc="ev")
            ph.op("scalar", lambda e: e.copy(bchb[:], bch[:]),
                  deps=[t_bc2], inc="sc")
            ph.run()

    def emit_prep(ph, wstg, wbf, tpb, nslot=1):
        conv = {}
        stor = {}
        lastw = {}
        ci = 0
        for (wsrc, wdst, nch, cw) in ((wg_d, wg16, 24, D), (wu_d, wu16, 24, D),
                                      (wd_d, wd16, 8, HID)):
            for ch in range(nch):
                bdep2 = None
                if cw == D:
                    off = (ci % 2) * 1024
                    bdep = stor.get(ci - 2)
                else:
                    off = 0
                    bdep = stor.get(ci - 1)
                    bdep2 = stor.get(ci - 2)
                soff = (ci % nslot) * HID
                t_l = ph.op("gpsimd", lambda e, s=wsrc, c=ch, cw=cw, soff=soff: e.dma_start(
                    wstg[:, soff:soff + cw],
                    s.ap()[c * 128:(c + 1) * 128, :]),
                    deps=[conv.get(ci - nslot)], inc=ph.st(f"pw{ci % nslot}"), n=16)
                conv[ci] = ph.op("vector", lambda e, cw=cw, off=off, soff=soff: e.tensor_scalar_mul(
                    wbf[:, off: off + cw], wstg[:, soff:soff + cw], QS),
                    deps=[t_l, bdep, bdep2], inc="ev")
                stor[ci] = ph.op("gpsimd", lambda e, d=wdst, c=ch, cw=cw, off=off: e.dma_start(
                    d.ap()[c * 128:(c + 1) * 128, :], wbf[:, off: off + cw]),
                    deps=[conv[ci]], inc=ph.st(f"pb{(ci % 2) if cw == D else 0}"), n=16)
                ci += 1
            lastw[len(lastw)] = [stor[ci - 1], stor.get(ci - 2)]
        tlast = {}
        ready = []
        ti_ = 0
        for wi, (wsrc16, wdstT) in enumerate(((wg16, wgT), (wu16, wuT))):
            for dc in range(8):
                sl = ti_ % 2
                t_t = ph.op("scalar", lambda e, s=wsrc16, dc=dc, sl=sl: e.dma_start(
                    tpb[:, sl * HID:(sl + 1) * HID],
                    s.ap()[:, dc * 128:(dc + 1) * 128], transpose=True),
                    deps=[tlast.get(sl)] + lastw[wi], inc=ph.sd(f"pt{sl}"), n=16)
                tlast[sl] = ph.op("gpsimd", lambda e, d=wdstT, dc=dc, sl=sl: e.dma_start(
                    d.ap().rearrange("(a b p h) -> b p a h", a=12, b=8, p=128)[dc],
                    tpb[:, sl * HID:(sl + 1) * HID].rearrange("p (a h) -> p a h", a=12)),
                    deps=[t_t], inc=ph.st(f"pp{sl}"), n=16)
                ti_ += 1
            ready.append([tlast.get(0), tlast.get(1)])
        for hc in range(24):
            sl = ti_ % 2
            t_t = ph.op("scalar", lambda e, hc=hc, sl=sl: e.dma_start(
                tpb[:, sl * HID: sl * HID + D],
                wd16.ap()[:, hc * 128:(hc + 1) * 128], transpose=True),
                deps=[tlast.get(sl)] + lastw[2], inc=ph.sd(f"pt{sl}"), n=16)
            tlast[sl] = ph.op("gpsimd", lambda e, hc=hc, sl=sl: e.dma_start(
                wdT.ap()[hc * 128:(hc + 1) * 128, :],
                tpb[:, sl * HID: sl * HID + D]),
                deps=[t_t], inc=ph.st(f"pp{sl}"), n=16)
            ti_ += 1
        ready.append([tlast.get(0), tlast.get(1)])
        return ready

    def ffn_phase(name, src_ap, dst_ap, with_prep=False, ext_stats=False):
        ph = Phase(P, name)
        wready = [[], [], []]
        pes = ExitStack()
        if with_prep:
            wstg = pes.enter_context(nc.sbuf_tensor(name + "_pws", [128, HID], I32))
            wbf = pes.enter_context(nc.sbuf_tensor(name + "_pwb", [128, HID + 1024], BF16))
            tpb = pes.enter_context(nc.sbuf_tensor(name + "_pwt", [128, 2 * HID], BF16))
        nslots = 8
        with (
            nc.sbuf_tensor(name + "_xst", [128, nslots * 1024], F32) as xst,
            nc.sbuf_tensor(name + "_xs2", [128, 4 * 1024], F32) as xst2,
            nc.sbuf_tensor(name + "_xntm", [128, 4 * 1024], BF16) as xntm,
            nc.sbuf_tensor(name + "_xnfm", [128, 8 * 1024], BF16) as xnfm,
            nc.sbuf_tensor(name + "_hsb", [128, 24 * 1024], BF16) as hsb,
            nc.sbuf_tensor(name + "_wslg", [128, 2 * 2048], BF16) as wslg,
            nc.sbuf_tensor(name + "_wslu", [128, 2 * 2048], BF16) as wslu,
            nc.sbuf_tensor(name + "_wdsb", [128, 24 * 1024], BF16) as wdsb,
            nc.sbuf_tensor(name + "_sq", [128, 1024], BF16) as sqscr,
            nc.sbuf_tensor(name + "_sg", [128, 2 * 512], BF16) as sgb,
            nc.sbuf_tensor(name + "_xo", [128, (1 if with_prep else 2) * 1024], F32) as x1o,
        ):
            if with_prep:
                wready = emit_prep(ph, wstg, wbf, tpb)
            t_wd_holder = [None]

            t_load = [None] * NT
            t_xn = [None] * NT
            t_ev = [None] * NT
            t_sq = [None] * NT
            t_red = [None] * NT
            t_tp = [None] * NT
            t_statsdone = [None] * 4
            t_evac_bank = [None, None]
            t_gu_mm = {}
            t_silu = {}
            t_hmul = {}
            t_down = {}
            t_stt = {}
            t_store = [None] * NT
            t_slab = {}
            gu_last = [None, None]

            if ext_stats:
                for b_ in range(3):
                    t_statsdone[b_] = stats_chain(
                        ph, b_, None, None,
                        t_statsdone[b_ - 1] if b_ >= 1 else None,
                        bufs=(sm2_s, sm2_q, sm2_m, sm2_r))

            for g in range(2):
                for ti in range(8):
                    tt = g * 8 + ti
                    slot = tt % nslots
                    if tt < 12:
                        t_load[tt] = ph.op("sync", lambda e, tt=tt, slot=slot: e.dma_start(
                            xst[:, slot * 1024:(slot + 1) * 1024],
                            src_ap[tt * 128:(tt + 1) * 128, :]),
                            deps=([t_xn[tt - 8],
                                   None if ext_stats else t_sq[tt - 8],
                                   t_stt.get((0, tt - 8, 1))]
                                  if tt >= 8 else []),
                            inc=ph.ld(f"x{slot}"), n=16)
                    else:
                        t_load[tt] = ph.op("sync", lambda e, tt=tt: e.dma_start(
                            xst2[:, (tt - 12) * 1024:(tt - 11) * 1024],
                            src_ap[tt * 128:(tt + 1) * 128, :]),
                            inc=ph.ld(f"y{tt - 12}"), n=16)
                    xs_ = xst2 if tt >= 12 else xst
                    xo_ = (tt - 12 if tt >= 12 else slot) * 1024
                    if not ext_stats or tt >= 12:
                        t_red[tt] = ph.op("vector", lambda e, tt=tt, xs_=xs_, xo_=xo_: e.reduce_sum(
                            sm_s[:, tt:tt + 1], xs_[:, xo_:xo_ + 1024],
                            axis=AX.X), deps=[t_load[tt]], inc="ev")
                        t_sq[tt] = ph.op("scalar", lambda e, tt=tt, xs_=xs_, xo_=xo_: e.activation(
                            sqscr[:], xs_[:, xo_:xo_ + 1024], AF.Square,
                            accum_out=sm_q[:, tt:tt + 1]),
                            deps=[t_load[tt],
                                  t_sq[tt - 1]
                                  if (tt >= 1 and (not ext_stats or tt >= 13))
                                  else None],
                            inc="sc")
                        if tt % 4 == 3:
                            b_ = tt // 4
                            t_statsdone[b_] = stats_chain(
                                ph, b_, t_red, t_sq,
                                t_statsdone[b_ - 1] if b_ >= 1 else None,
                                bufs=((sm_s, sm_q, sm2_m, sm2_r) if ext_stats
                                      else None))

                for ti in range(8):
                    tt = g * 8 + ti
                    slot = tt % nslots
                    xs_ = xst2 if tt >= 12 else xst
                    xo_ = (tt - 12 if tt >= 12 else slot) * 1024
                    mm2 = sm2_m if ext_stats else sm_m
                    rr2 = sm2_r if ext_stats else sm_r
                    t_xn[tt] = ph.op("vector", lambda e, tt=tt, xs_=xs_, xo_=xo_,
                                     mm2=mm2, rr2=rr2: e.tensor_scalar(
                        out=xntm[:, (tt % 4) * 1024:(tt % 4 + 1) * 1024],
                        in0=xs_[:, xo_:xo_ + 1024],
                        scalar1=mm2[:, tt:tt + 1], scalar2=rr2[:, tt:tt + 1],
                        op0=ALU.subtract, op1=ALU.mult),
                        deps=[t_statsdone[tt // 4], t_load[tt],
                              t_tp[tt - 4] if tt >= 4 else None],
                        inc="ev")
                    bank = tt % 2
                    tview = PS0[:, bank * 512:(bank + 1) * 512].bitcast(BF16)

                    def tp_fn(e, tt=tt, tview=tview):
                        for dc in range(8):
                            inst = e.transpose(
                                tview[:, dc * 128:(dc + 1) * 128],
                                xntm[:, (tt % 4) * 1024 + dc * 128:
                                     (tt % 4) * 1024 + (dc + 1) * 128],
                                ident[:])
                        return inst

                    t_tp[tt] = ph.op("tensor", tp_fn,
                                     deps=[t_xn[tt], t_evac_bank[bank]], inc="mm")
                    t_ev[tt] = ph.op("vector", lambda e, tview=tview, ti=ti: e.tensor_copy(
                        xnfm[:].rearrange("p (a t) -> p a t", a=8)[:, :, ti * 128:(ti + 1) * 128],
                        tview[:].rearrange("p (a t) -> p a t", a=8)),
                        deps=[t_tp[tt], gu_last[g - 1] if g >= 1 else None],
                        inc="ev")
                    t_evac_bank[bank] = t_ev[tt]

                grp_ev_h = {0: t_ev[g * 8 + 3], 1: t_ev[g * 8 + 7]}

                for hc2 in range(12):
                    sl = hc2 % 2
                    if g == 0 and hc2 < 2:
                        slab_dep = None
                    elif hc2 >= 2:
                        slab_dep = t_gu_mm.get((g, hc2 - 2, 1, 1))
                    else:
                        slab_dep = t_gu_mm.get((g - 1, 10 + hc2, 1, 1))
                    t_lg = ph.op("sync", lambda e, hc2=hc2, sl=sl: e.dma_start(
                        wslg[:, sl * 2048:(sl + 1) * 2048].rearrange("p (b h) -> p b h", b=8),
                        wgT.ap().rearrange("(a b p h) -> a p b h", a=12, b=8, p=128)[hc2]),
                        deps=[slab_dep] + wready[0], inc=ph.ld(f"gu{sl}"), n=16)
                    t_lu = ph.op("sync", lambda e, hc2=hc2, sl=sl: e.dma_start(
                        wslu[:, sl * 2048:(sl + 1) * 2048].rearrange("p (b h) -> p b h", b=8),
                        wuT.ap().rearrange("(a b p h) -> a p b h", a=12, b=8, p=128)[hc2]),
                        deps=[slab_dep] + wready[1], inc=ph.ld(f"gu{sl}"), n=16)
                    for hcl in range(2):
                        hc = hc2 * 2 + hcl
                        for th in range(2):
                            k = hc * 2 + th
                            par = k % 2
                            gview = PS0[:, 1024:1536] if par == 0 else PS1[:, 0:512]
                            uview = PS0[:, 1536:2048] if par == 0 else PS1[:, 512:1024]

                            def gu_fn(e, hcl=hcl, th=th, sl=sl, gview=gview, uview=uview):
                                for dc in range(8):
                                    lg = wslg[:, sl * 2048 + dc * 256 + hcl * 128:
                                              sl * 2048 + dc * 256 + (hcl + 1) * 128]
                                    rhs = xnfm[:, dc * 1024 + th * 512:
                                               dc * 1024 + (th + 1) * 512]
                                    e.matmul(gview, lg, rhs, start=(dc == 0), stop=(dc == 7))
                                for dc in range(8):
                                    lu = wslu[:, sl * 2048 + dc * 256 + hcl * 128:
                                              sl * 2048 + dc * 256 + (hcl + 1) * 128]
                                    rhs = xnfm[:, dc * 1024 + th * 512:
                                               dc * 1024 + (th + 1) * 512]
                                    inst = e.matmul(uview, lu, rhs, start=(dc == 0), stop=(dc == 7))
                                return inst

                            t_mmgu = ph.op("tensor", gu_fn,
                                           deps=[t_lg, t_lu, grp_ev_h[th],
                                                 t_silu.get((g, k - 2)),
                                                 t_hmul.get((g, k - 2))],
                                           inc="mm")
                            t_gu_mm[(g, hc2, hcl, th)] = t_mmgu
                            gu_last[g] = t_mmgu
                            t_silu[(g, k)] = ph.op(
                                "scalar", lambda e, gview=gview, th=th: e.activation(
                                    sgb[:, (th % 2) * 512:(th % 2 + 1) * 512],
                                    gview, AF.Silu),
                                deps=[t_mmgu, t_hmul.get((g, k - 2)),
                                      t_silu.get((g, k - 2))], inc="sc")
                            t_hmul[(g, k)] = ph.op(
                                "vector", lambda e, uview=uview, hc=hc, th=th: e.tensor_mul(
                                    hsb[:, hc * 1024 + th * 512: hc * 1024 + (th + 1) * 512],
                                    uview, sgb[:, (th % 2) * 512:(th % 2 + 1) * 512]),
                                deps=[t_silu[(g, k)], t_mmgu,
                                      t_down.get((g - 1, 7, 1))], inc="ev")

                hmul_done = t_hmul[(g, 47)]
                if g == 0:
                    t_wd_holder[0] = ph.op("sync", lambda e: e.dma_start(
                        wdsb[:].rearrange("p (a h) -> p a h", a=24),
                        wdT.ap().rearrange("(a p) h -> p a h", p=128)),
                        deps=wready[2], inc=ph.ld("wd"), n=16)
                t_wd = t_wd_holder[0]

                for ti in range(8):
                    tt = g * 8 + ti
                    for oh in range(2):
                        dview = PS1[:, 1024 + oh * 512: 1536 + oh * 512]

                        def dn_fn(e, ti=ti, oh=oh, dview=dview):
                            for hc in range(24):
                                inst = e.matmul(
                                    dview,
                                    hsb[:, hc * 1024 + ti * 128: hc * 1024 + (ti + 1) * 128],
                                    wdsb[:, hc * 1024 + oh * 512: hc * 1024 + (oh + 1) * 512],
                                    start=(hc == 0), stop=(hc == 23))
                            return inst

                        t_down[(g, ti, oh)] = ph.op(
                            "tensor", dn_fn,
                            deps=[hmul_done, t_wd if (g == 0) else None,
                                  t_stt.get((g, ti - 1, oh)),
                                  t_stt.get((g - 1, 7, oh))],
                            inc="mm")
                        xom = 1 if with_prep else 2
                        rs_ = xst2 if tt >= 12 else xst
                        ro_ = (tt - 12 if tt >= 12 else tt % 8) * 1024
                        t_stt[(g, ti, oh)] = ph.op(
                            "vector", lambda e, ti=ti, oh=oh, dview=dview, xom=xom,
                            rs_=rs_, ro_=ro_: e.scalar_tensor_tensor(
                                out=x1o[:, (ti % xom) * 1024 + oh * 512:
                                        (ti % xom) * 1024 + (oh + 1) * 512],
                                in0=dview, scalar=0.5,
                                in1=rs_[:, ro_ + oh * 512:ro_ + (oh + 1) * 512],
                                op0=ALU.mult, op1=ALU.add),
                            deps=[t_down[(g, ti, oh)],
                                  t_store[tt - xom] if tt >= xom else None],
                            inc="ev")
                    t_store[tt] = ph.op("gpsimd", lambda e, tt=tt, ti=ti, xom=(1 if with_prep else 2): e.dma_start(
                        dst_ap[tt * 128:(tt + 1) * 128, :],
                        x1o[:, (ti % xom) * 1024:(ti % xom + 1) * 1024]),
                        deps=[t_stt[(g, ti, 0)], t_stt[(g, ti, 1)]],
                        inc=ph.st(f"o{ti % 2}"), n=16)

            flush(ph)
            ph.run()
        pes.close()

    def qkv_phase(seq, K_sb, Q_sb, V_sb):
        nm = f"a2s{seq}"
        ph = Phase(P, nm)
        with (
            nc.sbuf_tensor(nm + "_Wq", [128, 8 * 1024], BF16) as Wqs,
            nc.sbuf_tensor(nm + "_Wk", [128, 8 * 1024], BF16) as Wks,
            nc.sbuf_tensor(nm + "_Wv", [128, 8 * 1024], BF16) as Wvs,
            nc.sbuf_tensor(nm + "_bvec", [1, 1024], F32) as bvec,
            nc.sbuf_tensor(nm + "_bb16", [1, 3 * 1024], BF16) as bb16,
            nc.sbuf_tensor(nm + "_xst", [128, 6 * 1024], F32) as xst2,
            nc.sbuf_tensor(nm + "_xntm", [128, 2 * 1024], BF16) as xntm2,
            nc.sbuf_tensor(nm + "_xnfm", [128, 2 * 8 * 512], BF16) as xnfm2,
            nc.sbuf_tensor(nm + "_sq", [128, 1024], BF16) as sqscr2,
        ):
            cast = {}
            ci = 0
            for wi, (wsrc, wdst) in enumerate(((Wq_d, Wqs), (Wk_d, Wks), (Wv_d, Wvs))):
                for dc in range(8):
                    cast[ci] = ph.op("gpsimd", lambda e, s=wsrc, d=wdst, dc=dc: e.dma_start(
                        d[:, dc * 1024:(dc + 1) * 1024],
                        s.ap()[dc * 128:(dc + 1) * 128, :]),
                        inc=ph.st(f"c{wi}"), n=16)
                    ci += 1
            t_ba_prev = None
            t_bc_prev = None
            t_fold = {}
            for wi, (wsb, bsrc) in enumerate(((Wqs, bq_d), (Wks, bk_d), (Wvs, bv_d))):
                t_lb = ph.op("sync", lambda e, s=bsrc: e.dma_start(
                    bvec[0:1, :], s.ap().rearrange("(a d) -> a d", a=1)),
                    deps=[t_ba_prev], inc=ph.ld("bv"), n=16)
                def bmm(e, wsb=wsb):
                    for oi in range(2):
                        for dc in range(8):
                            inst = e.matmul(
                                PS0[0:1, oi * 512:(oi + 1) * 512],
                                bchb[:, dc:dc + 1],
                                wsb[:, dc * 1024 + oi * 512: dc * 1024 + (oi + 1) * 512],
                                start=(dc == 0), stop=(dc == 7))
                    return inst

                t_bm = ph.op("tensor", bmm,
                             deps=[cast[wi * 8 + dc] for dc in range(8)] + [t_ba_prev],
                             inc="mm")
                t_ba = ph.op("vector", lambda e, wi=wi: e.tensor_add(
                    bb16[0:1, wi * 1024:(wi + 1) * 1024],
                    PS0[0:1, 0:1024], bvec[0:1, :]),
                    deps=[t_bm, t_lb], inc="ev")
                t_ba_prev = t_ba
                t_fold[("b", wi)] = t_ba
                for dc in range(8):
                    t_fold[(wi, dc)] = ph.op(
                        "vector", lambda e, wsb=wsb, dc=dc: e.tensor_scalar_mul(
                            wsb[:, dc * 1024:(dc + 1) * 1024],
                            wsb[:, dc * 1024:(dc + 1) * 1024],
                            gch[:, dc:dc + 1]),
                        deps=[t_bm], inc="ev", pre_nops=1)
            t_bias_done = t_fold[("b", 2)]

            def vones(e):
                for t in range(16):
                    inst = e.memset(
                        V_sb[:, t * 1088:(t + 1) * 1088].rearrange(
                            "p (h c) -> p h c", h=16)[:, :, 64:65], 1.0)
                return inst
            t_vones = ph.op("gpsimd", vones, inc="gp")

            t_load = [None] * NT
            t_red = [None] * NT
            t_sq = [None] * NT
            t_xn = [None] * NT
            t_tp = [None] * NT
            t_ev = [None] * NT
            t_statsdone = [None] * 4
            t_evac_bank = [None, None]
            t_qk = {}
            t_qke = {}
            t_v = {}
            t_ve = {}

            for g4 in range(4):
                for ti in range(4):
                    tt = g4 * 4 + ti
                    slot = tt % 6
                    t_load[tt] = ph.op("sync", lambda e, tt=tt, slot=slot: e.dma_start(
                        xst2[:, slot * 1024:(slot + 1) * 1024],
                        x1s.ap()[tt * 128:(tt + 1) * 128, :]),
                        deps=[t_xn[tt - 6], t_sq[tt - 6]] if tt >= 6 else [],
                        inc=ph.ld(f"x{slot}"), n=16)
                    t_red[tt] = ph.op("vector", lambda e, tt=tt, slot=slot: e.reduce_sum(
                        sm_s[:, tt:tt + 1], xst2[:, slot * 1024:(slot + 1) * 1024],
                        axis=AX.X), deps=[t_load[tt]], inc="ev")
                    t_sq[tt] = ph.op("scalar", lambda e, tt=tt, slot=slot: e.activation(
                        sqscr2[:], xst2[:, slot * 1024:(slot + 1) * 1024], AF.Square,
                        accum_out=sm_q[:, tt:tt + 1]),
                        deps=[t_load[tt], t_sq[tt - 1] if tt >= 1 else None],
                        inc="sc")
                    if tt % 4 == 3:
                        b_ = tt // 4
                        t_statsdone[b_] = stats_chain(
                            ph, b_, t_red, t_sq,
                            t_statsdone[b_ - 1] if b_ >= 1 else None)

                for ti in range(4):
                    tt = g4 * 4 + ti
                    slot = tt % 6
                    t_xn[tt] = ph.op("vector", lambda e, tt=tt, slot=slot: e.tensor_scalar(
                        out=xntm2[:, (tt % 2) * 1024:(tt % 2 + 1) * 1024],
                        in0=xst2[:, slot * 1024:(slot + 1) * 1024],
                        scalar1=sm_m[:, tt:tt + 1], scalar2=sm_r[:, tt:tt + 1],
                        op0=ALU.subtract, op1=ALU.mult),
                        deps=[t_statsdone[tt // 4],
                              t_tp[tt - 2] if tt >= 2 else None],
                        inc="ev")
                    bank = tt % 2
                    tview = PS0[:, bank * 512:(bank + 1) * 512].bitcast(BF16)

                    def tp_fn(e, tt=tt, tview=tview):
                        for dc in range(8):
                            inst = e.transpose(
                                tview[:, dc * 128:(dc + 1) * 128],
                                xntm2[:, (tt % 2) * 1024 + dc * 128:
                                      (tt % 2) * 1024 + (dc + 1) * 128],
                                ident[:])
                        return inst

                    t_tp[tt] = ph.op("tensor", tp_fn,
                                     deps=[t_xn[tt], t_evac_bank[bank],
                                           t_bias_done if tt <= 1 else None],
                                     inc="mm")
                    gp_ = (tt // 4) % 2
                    t_ev[tt] = ph.op("vector", lambda e, tview=tview, ti=ti, gp_=gp_: e.tensor_copy(
                        xnfm2[:, gp_ * 4096:(gp_ + 1) * 4096].rearrange(
                            "p (a t) -> p a t", a=8)[:, :, ti * 128:(ti + 1) * 128],
                        tview[:].rearrange("p (a t) -> p a t", a=8)),
                        deps=[t_tp[tt], t_v.get((g4 - 2, 3, 1))], inc="ev")
                    t_evac_bank[tt % 2] = t_ev[tt]

                grp_ev = t_ev[g4 * 4 + 3]
                fold_deps = [t_fold[(wi, dc)] for wi in range(3) for dc in range(8)] \
                    if g4 == 0 else []

                for pi, (wsb, dst_sb) in enumerate(((Wqs, Q_sb), (Wks, K_sb))):
                    for oc in range(8):
                        par = oc % 2
                        qview = PS0[:, 1024 + par * 512: 1536 + par * 512]

                        gp_ = g4 % 2

                        def qk_fn(e, wsb=wsb, oc=oc, pi=pi, qview=qview, gp_=gp_):
                            for dc in range(8):
                                e.matmul(qview,
                                         wsb[:, dc * 1024 + oc * 128:
                                             dc * 1024 + (oc + 1) * 128],
                                         xnfm2[:, gp_ * 4096 + dc * 512:
                                               gp_ * 4096 + (dc + 1) * 512],
                                         start=(dc == 0), stop=False)
                            inst = e.matmul(qview,
                                            bb16[0:1, pi * 1024 + oc * 128:
                                                 pi * 1024 + (oc + 1) * 128],
                                            ones_r[0:1, 0:512], start=False, stop=True)
                            return inst

                        if pi == 1 and oc < 2:
                            bank_dep = t_qke.get((g4, 0, 6 + par))
                        else:
                            bank_dep = t_qke.get((g4, pi, oc - 2))
                        if pi == 0 and oc < 2:
                            bank_dep = t_qke.get((g4 - 1, 1, 6 + par))
                        t_qk[(g4, pi, oc)] = ph.op(
                            "tensor", qk_fn,
                            deps=[grp_ev, bank_dep] + fold_deps,
                            inc="mm")
                        fold_deps = []
                        t_qke[(g4, pi, oc)] = ph.op(
                            "vector", lambda e, d=dst_sb, oc=oc, g4=g4, qview=qview: e.tensor_copy(
                                d[:, oc * 2048 + g4 * 512: oc * 2048 + (g4 + 1) * 512],
                                qview),
                            deps=[t_qk[(g4, pi, oc)]], inc="ev")

                for ti in range(4):
                    tt = g4 * 4 + ti
                    for ov in range(2):
                        vview = PS1[:, ov * 512:(ov + 1) * 512]
                        if ti == 0:
                            vdep = t_ve.get((g4 - 1, 3, ov))
                        else:
                            vdep = t_ve.get((g4, ti - 1, ov))

                        gp_ = g4 % 2

                        def v_fn(e, ti=ti, ov=ov, vview=vview, gp_=gp_):
                            for dc in range(8):
                                e.matmul(vview,
                                         xnfm2[:, gp_ * 4096 + dc * 512 + ti * 128:
                                               gp_ * 4096 + dc * 512 + (ti + 1) * 128],
                                         Wvs[:, dc * 1024 + ov * 512:
                                             dc * 1024 + (ov + 1) * 512],
                                         start=(dc == 0), stop=False)
                            inst = e.matmul(vview, ones_r[0:1, 0:128],
                                            bb16[0:1, 2048 + ov * 512:
                                                 2048 + (ov + 1) * 512],
                                            start=False, stop=True)
                            return inst

                        t_v[(g4, ti, ov)] = ph.op(
                            "tensor", v_fn, deps=[grp_ev, vdep], inc="mm")
                        t_ve[(g4, ti, ov)] = ph.op(
                            "vector", lambda e, tt=tt, ov=ov, vview=vview: e.tensor_copy(
                                V_sb[:, tt * 1088 + ov * 8 * 68:
                                     tt * 1088 + (ov * 8 + 8) * 68].rearrange(
                                         "p (h c) -> p h c", h=8)[:, :, 0:64],
                                vview.rearrange("p (h c) -> p h c", h=8)),
                            deps=[t_v[(g4, ti, ov)], t_vones], inc="ev")

            flush(ph)
            ph.run()

    def emit_norms(ph, K_sb, Q_sb, nsq):
        t_red = {}
        for pi, src_sb in enumerate((Q_sb, K_sb)):
            t_sq = {}
            t_mm = {}
            for dc in range(8):
                sl = dc % 2
                t_sq[dc] = ph.op("vector", lambda e, s=src_sb, dc=dc, sl=sl: e.tensor_mul(
                    nsq[:, sl * 2048:(sl + 1) * 2048],
                    s[:, dc * 2048:(dc + 1) * 2048],
                    s[:, dc * 2048:(dc + 1) * 2048]),
                    deps=[t_mm.get(dc - 2), t_red.get(pi - 1),
                          t_sq.get(dc - 1)], inc="ev")

                def nmm(e, dc=dc, sl=sl):
                    for j in range(4):
                        inst = e.matmul(
                            PS0[0:16, j * 512:(j + 1) * 512],
                            selp[:, dc * 16:(dc + 1) * 16],
                            nsq[:, sl * 2048 + j * 512: sl * 2048 + (j + 1) * 512],
                            start=(dc == 0), stop=(dc == 7))
                    return inst

                t_mm[dc] = ph.op("tensor", nmm,
                                 deps=[t_sq[dc], t_red.get(pi - 1)], inc="mm")
            t_red[pi] = ph.op("vector", lambda e, pi=pi: e.reduce_max(
                n16[:, pi:pi + 1],
                PS0[0:16, :].rearrange("p (a t) -> p a t", a=4), axis=AX.XY),
                deps=[t_mm[7]], inc="ev")
        t_p = ph.op("vector", lambda e: e.tensor_mul(
            n16[:, 2:3], n16[:, 0:1], n16[:, 1:2]),
            deps=[t_red[1]], inc="ev", pre_drain=True)
        t_s = ph.op("scalar", lambda e: e.activation(
            n16[:, 3:4], n16[:, 2:3], AF.Sqrt), deps=[t_p], inc="sc")
        t_c = ph.op("vector", lambda e: e.tensor_scalar_mul(
            n16[:, 4:5], n16[:, 3:4], -1.01 * 0.125),
            deps=[t_s], inc="ev")
        t_t = ph.op("tensor", lambda e: e.matmul(
            PS1[0:1, 0:16], n16[0:16, 4:5], id16[:], start=True, stop=True),
            deps=[t_c], inc="mm")
        t_cr = ph.op("vector", lambda e: e.tensor_copy(
            crow[0:1, :], PS1[0:1, 0:16]), deps=[t_t], inc="ev")
        t_pm = ph.op("vector", lambda e: e.tensor_tensor(
            crow2[0:1, :],
            crow[0:1, :].rearrange("a (h two) -> a h two", two=2)[:, :, 0],
            crow[0:1, :].rearrange("a (h two) -> a h two", two=2)[:, :, 1],
            op=ALU.min), deps=[t_cr], inc="ev", pre_drain=True)
        t_bc = ph.op("tensor", lambda e: e.matmul(
            PS1[:, 512:520], ones_r[0:1, 0:128], crow2[0:1, :],
            start=True, stop=True), deps=[t_pm], inc="mm")
        return ph.op("vector", lambda e: e.tensor_copy(cb8[:], PS1[:, 512:520]),
                     deps=[t_bc], inc="ev")

    def attn_phase(seq, K_sb, Q_sb, V_sb):
        nm = f"att{seq}"
        ph = Phase(P, nm)
        with (
            nc.sbuf_tensor(nm + "_nsq", [128, 2 * 2048], BF16) as nsq,
            nc.sbuf_tensor(nm + "_eS", [128, 2 * 16 * 1024], BF16) as expS,
            nc.sbuf_tensor(nm + "_x1q", [128, 2 * 1024], F32) as x1q,
            nc.sbuf_tensor(nm + "_x2b", [128, 4 * 1024], BF16) as x2b,
            nc.sbuf_tensor(nm + "_x2o", [128, 4 * 1024], F32) as x2o,
            nc.sbuf_tensor(nm + "_lb", [128, 16], F32) as lbuf,
        ):
            t_cb = emit_norms(ph, K_sb, Q_sb, nsq)
            t_x2add = {}
            t_store = {}
            t_st2 = {}
            t_exp = {}
            t_pvc = {}
            t_yev = {}
            t_x1 = {}
            pv_war = {0: t_cb, 1: t_cb}
            exp_last = {0: t_cb, 1: t_cb}
            pend = [None]

            def emit_pv_chunk(qg, hp, c):
                buf = (hp % 2) * 16384
                qi = c // 2
                half = c % 2
                qtp = qi // 2
                par = qi % 2
                h = hp * 2 + par

                def pv_fn(e, h=h, par=par, qtp=qtp, half=half, buf=buf):
                    for kt in range(half * 8, half * 8 + 8):
                        for q2 in range(2):
                            qt = qtp * 2 + q2
                            inst = e.matmul(
                                PS1[:, (par * 2 + q2) * 512:
                                     (par * 2 + q2) * 512 + 65],
                                expS[:, buf + kt * 1024 + par * 512 + qt * 128:
                                     buf + kt * 1024 + par * 512 + (qt + 1) * 128],
                                V_sb[:, kt * 1088 + h * 68: kt * 1088 + h * 68 + 65],
                                start=(kt == 0), stop=(kt == 15))
                    return inst

                deps = [t_exp[(qg, hp, 15)]]
                if half == 0 and par == 0:
                    if qtp == 1:
                        deps.append(t_yev.get((qg, hp, 0)))
                    else:
                        pq = (qg, hp - 1) if hp >= 1 else (qg - 1, 7)
                        deps.append(t_yev.get((pq[0], pq[1], 1)))
                        deps.append(t_cb)
                tok = ph.op("tensor", pv_fn, deps=deps, inc="mm")
                t_pvc[(qg, hp, c)] = tok
                pv_war[hp % 2] = tok
                if c in (3, 7):
                    qtp = (c - 3) // 4
                    pvdeps = [t_pvc[(qg, hp, qtp * 4 + k)] for k in range(4)]
                    t_l = ph.op("vector", lambda e, qtp=qtp: e.tensor_scalar(
                        out=lbuf[:, qtp * 4: qtp * 4 + 4][:, :, None],
                        in0=PS1[:, :].rearrange("p (a t) -> p a t", a=4)[:, :, 64:65],
                        scalar1=1e-30, scalar2=None, op0=ALU.max),
                        deps=pvdeps, inc="ev")
                    t_r = ph.op("vector", lambda e, qtp=qtp: e.reciprocal(
                        lbuf[:, 8 + qtp * 4: 8 + qtp * 4 + 4],
                        lbuf[:, qtp * 4: qtp * 4 + 4]),
                        deps=[t_l], inc="ev", pre_drain=True)
                    t_yev[(qg, hp, qtp)] = ph.op(
                        "vector", lambda e, hp=hp, qtp=qtp: e.tensor_mul(
                            x2b[:].rearrange("p (q2 hh t) -> p q2 hh t", q2=4, hh=16)[
                                :, 2 * qtp:2 * qtp + 2, 2 * hp:2 * hp + 2, :]
                            .transpose([0, 2, 1, 3]),
                            PS1[:, :].rearrange("p (par q2 t) -> p par q2 t",
                                                par=2, q2=2)[:, :, :, 0:64],
                            lbuf[:, 8 + qtp * 4: 8 + qtp * 4 + 4]
                            .rearrange("p (par q2) -> p par q2", par=2)[:, :, :, None]
                            .broadcast_to([128, 2, 2, 64])),
                        deps=[t_r, t_x2add.get((qg - 1, 3))], inc="ev")

            def drain_qg(qg):
                for qt in range(4):
                    fi = qg * 4 + qt
                    if qt == 2:
                        t_x1[(qg, 1)] = ph.op("sync", lambda e, qg=qg: e.dma_start(
                            x1q[:].rearrange("p (a t) -> p a t", a=2),
                            x1s.ap().rearrange("(a p) t -> p a t", p=128)[
                                :, qg * 4 + 2: qg * 4 + 4, :]),
                            deps=[t_x2add[(qg, 1)]], inc=ph.ld("xq"), n=16)
                    xsl = x2o[:, qt * 1024:(qt + 1) * 1024]
                    t_x2add[(qg, qt)] = ph.op("vector", lambda e, xsl=xsl, qt=qt: e.tensor_add(
                        xsl,
                        x2b[:, qt * 1024:(qt + 1) * 1024],
                        x1q[:, (qt % 2) * 1024:(qt % 2 + 1) * 1024]),
                        deps=[t_yev[(qg, 7, qt // 2)], t_x1[(qg, qt // 2)],
                              t_store.get(fi - 4), t_st2.get(fi - 4)], inc="ev")
                    t_store[fi] = ph.op("gpsimd", lambda e, fi=fi, qt=qt, xsl=xsl: e.dma_start(
                        x2s.ap()[fi * 128:(fi + 1) * 128, :], xsl),
                        deps=[t_x2add[(qg, qt)]], inc=ph.st(f"o{qt}"), n=16)

            def stats_push(qg, qt):
                fi = qg * 4 + qt
                xsl = x2o[:, qt * 1024:(qt + 1) * 1024]
                nsl = nsq[:, (qt % 2) * 1024:(qt % 2 + 1) * 1024]
                t_sm = ph.op("vector", lambda e, xsl=xsl, nsl=nsl: e.tensor_mul(
                    nsl, xsl, xsl),
                    deps=[t_x2add[(qg, qt)], t_cb, t_st2.get(fi - 2)],
                    inc="ev")
                ph.op("vector", lambda e, xsl=xsl, fi=fi: e.reduce_sum(
                    sm2_s[:, fi:fi + 1], xsl, axis=AX.X),
                    deps=[t_x2add[(qg, qt)]], inc="ev")
                t_st2[fi] = ph.op("vector", lambda e, nsl=nsl, fi=fi: e.reduce_sum(
                    sm2_q[:, fi:fi + 1], nsl, axis=AX.X),
                    deps=[t_sm], inc="ev")

            t_x1[(0, 0)] = ph.op("sync", lambda e: e.dma_start(
                x1q[:].rearrange("p (a t) -> p a t", a=2),
                x1s.ap().rearrange("(a p) t -> p a t", p=128)[:, 0:2, :]),
                inc=ph.ld("xq"), n=16)
            for qg in range(4):
                for hp in range(8):
                    buf = (hp % 2) * 16384
                    for r in range(16):
                        pp = (r % 2) * 1024

                        def st_fn(e, hp=hp, r=r, qg=qg, pp=pp):
                            for par in range(2):
                                inst = e.matmul(
                                    PS0[:, pp + par * 512: pp + (par + 1) * 512],
                                    K_sb[64 * par:64 * par + 64,
                                         hp * 2048 + r * 128:
                                         hp * 2048 + (r + 1) * 128],
                                    Q_sb[64 * par:64 * par + 64,
                                         hp * 2048 + qg * 512:
                                         hp * 2048 + (qg + 1) * 512],
                                    start=True, stop=True)
                            return inst

                        t_mm = ph.op("tensor", st_fn, deps=[exp_last[r % 2]],
                                     inc="mm")
                        t_exp[(qg, hp, r)] = ph.op(
                            "scalar", lambda e, hp=hp, r=r, buf=buf, pp=pp: e.activation(
                                expS[:, buf + r * 1024: buf + (r + 1) * 1024],
                                PS0[:, pp:pp + 1024], AF.Exp,
                                bias=cb8[:, hp:hp + 1], scale=0.125),
                            deps=[t_mm, pv_war[hp % 2], t_cb], inc="sc")
                        exp_last[r % 2] = t_exp[(qg, hp, r)]
                        if pend[0] is not None and r % 2 == 1:
                            emit_pv_chunk(pend[0][0], pend[0][1], r // 2)
                    pend[0] = (qg, hp)
                    if qg >= 1 and qg - 1 < 3 and hp in (1, 2, 3, 4):
                        stats_push(qg - 1, hp - 1)
                    if hp == 0 and qg > 0:
                        drain_qg(qg - 1)
                        t_x1[(qg, 0)] = ph.op("sync", lambda e, qg=qg: e.dma_start(
                            x1q[:].rearrange("p (a t) -> p a t", a=2),
                            x1s.ap().rearrange("(a p) t -> p a t", p=128)[
                                :, qg * 4: qg * 4 + 2, :]),
                            deps=[t_x2add[(qg - 1, 3)]], inc=ph.ld("xq"), n=16)
            for c in range(8):
                emit_pv_chunk(3, 7, c)
            drain_qg(3)
            flush(ph)
            ph.run()

    if "prep" in stages:
        prep()
        php = Phase(P, "prep_w")
        with (
            nc.sbuf_tensor("pp_ws", [128, 4 * HID], I32) as pws,
            nc.sbuf_tensor("pp_wb", [128, 4 * HID], BF16) as pwb,
            nc.sbuf_tensor("pp_wo", [128, 4 * HID], BF16) as pwo,
        ):
            conv = {}
            evac = {}
            stor = {}
            tps = {}
            ci = 0
            bank = [0]
            for (wsrc, wdstT, nch, cw) in ((wg_d, wgT, 24, D), (wu_d, wuT, 24, D),
                                           (wd_d, wdT, 8, HID)):
                ntile = cw // 128
                for ch in range(nch):
                    sl = ci % 4
                    t_l = php.op("sync", lambda e, s=wsrc, c=ch, sl=sl, cw=cw: e.dma_start(
                        pws[:, sl * HID: sl * HID + cw],
                        s.ap()[c * 128:(c + 1) * 128, :]),
                        deps=[conv.get(ci - 4)], inc=php.ld(f"w{sl}"), n=16)
                    conv[ci] = php.op("vector", lambda e, sl=sl, cw=cw: e.tensor_scalar_mul(
                        pwb[:, sl * HID: sl * HID + cw],
                        pws[:, sl * HID: sl * HID + cw], QS),
                        deps=[t_l, tps.get(ci - 4)], inc="ev")
                    nround = (ntile + 7) // 8
                    for rd in range(nround):
                        t0, t1 = rd * 8, min(ntile, rd * 8 + 8)
                        bk = bank[0] % 2
                        bank[0] += 1
                        tview = PS0[:, bk * 512:(bk + 1) * 512].bitcast(BF16)

                        def tp_fn(e, sl=sl, t0=t0, t1=t1, tview=tview):
                            for dc in range(t0, t1):
                                inst = e.transpose(
                                    tview[:, (dc - t0) * 128:(dc - t0 + 1) * 128],
                                    pwb[:, sl * HID + dc * 128: sl * HID + (dc + 1) * 128],
                                    ident[:])
                            return inst

                        t_tp = php.op("tensor", tp_fn,
                                      deps=[conv[ci], evac.get(bank[0] - 3)],
                                      inc="mm")
                        tps[ci] = t_tp
                        t_e = php.op("vector", lambda e, sl=sl, t0=t0, t1=t1, tview=tview: e.tensor_copy(
                            pwo[:, sl * HID + t0 * 128: sl * HID + t1 * 128],
                            tview[:, 0:(t1 - t0) * 128]),
                            deps=[t_tp, stor.get(ci - 4)], inc="ev")
                        evac[(ci, rd)] = t_e
                        evac[bank[0] - 1] = t_e
                    if cw == D:
                        def st_ap(c):
                            return bass.AP(
                                wdstT, (c // 2) * 262144 + (c % 2) * 128,
                                [[256, 128], [32768, 8], [1, 128]])
                        out_ap = st_ap(ch)
                    else:
                        out_ap = bass.AP(
                            wdT, ch * 128, [[1024, 128], [131072, 24], [1, 128]])
                    edeps = [evac[(ci, rd)] for rd in range(nround)]
                    stor[ci] = php.op("gpsimd", lambda e, sl=sl, cw=cw, out_ap=out_ap: e.dma_start(
                        out_ap,
                        pwo[:, sl * HID: sl * HID + cw].rearrange(
                            "p (a h) -> p a h", a=cw // 128)),
                        deps=edeps, inc=php.st(f"pb{sl}"), n=16)
                    ci += 1
            flush(php)
            php.run()
    for s in range(NSEQ):
        if "a1" in stages:
            ffn_phase(f"a1s{s}", x_d.ap()[s], x1s.ap())
            if debug and s == 0:
                phd = Phase(P, "dbgsm")
                for i, buf in enumerate((sm_s, sm_q, sm_m, sm_r)):
                    phd.op("gpsimd", lambda e, i=i, buf=buf: e.dma_start(
                        dbg_sm.ap()[:, i * 16:(i + 1) * 16], buf[:]),
                        inc=phd.st(f"d{i}"), n=16)
                flush(phd)
                phd.run()
        if {"a2", "norm", "b"} & set(stages):
            with (
                nc.sbuf_tensor(f"K{s}", [128, 8 * 2048], BF16) as K_sb,
                nc.sbuf_tensor(f"Q{s}", [128, 8 * 2048], BF16) as Q_sb,
                nc.sbuf_tensor(f"V{s}", [128, 16 * 1088], BF16) as V_sb,
            ):
                if "a2" in stages:
                    qkv_phase(s, K_sb, Q_sb, V_sb)
                if "b" in stages:
                    attn_phase(s, K_sb, Q_sb, V_sb)
        if "c" in stages:
            ffn_phase(f"cs{s}", x2s.ap(), out_d.ap()[s], ext_stats=True)
        if "one_seq" in stages:
            break

    ph = Phase(P, "fin")
    for i in range(8):
        ph.op("gpsimd", lambda e, l=f"st{i}": e.wait_ge(sems[l], P.C[l]))
    ph.run()

    es.close()
    return nc


_NC_CACHE = [None]


def kernel(**inputs):
    x = np.ascontiguousarray(np.asarray(inputs["L0_attn_x"], dtype=np.float32))
    full = {k: np.ascontiguousarray(np.asarray(v)) for k, v in inputs.items()
            if k != "L0_attn_x"}
    if _NC_CACHE[0] is None:
        _NC_CACHE[0] = build_program()
    nc = _NC_CACHE[0]
    in_maps = []
    for c in range(8):
        m = dict(full)
        m["L0_attn_x"] = x[c * NSEQ:(c + 1) * NSEQ]
        in_maps.append(m)
    res = run_bass_kernel_spmd(nc, in_maps, core_ids=list(range(8)))
    out = np.concatenate([np.asarray(r["out"]) for r in res.results], axis=0)
    return out.astype(np.float32)


# version 23
# speedup vs baseline: 1.0129x; 1.0129x over previous
import numpy as np
from contextlib import ExitStack

import concourse.bass as bass
import concourse.mybir as mybir
from concourse.bass_utils import run_bass_kernel_spmd

F32 = mybir.dt.float32
BF16 = mybir.dt.bfloat16
I32 = mybir.dt.int32
AF = mybir.ActivationFunctionType
ALU = mybir.AluOpType
AX = mybir.AxisListType

NSEQ = 2
T = 2048
D = 1024
H = 16
HID = 3072
NT = T // 128
EPS = 1e-5
QS = 1.0 / 127.0

ENGINES = ("sync", "gpsimd", "vector", "scalar", "tensor")


class Prog:
    def __init__(self, nc, sems):
        self.nc = nc
        self.sems = sems
        self.C = {k: 0 for k in sems}
        self.epoch = 0
        self.waited = {}


class Phase:
    def __init__(self, prog, name):
        self.p = prog
        self.name = name
        self.q = {e: [] for e in ENGINES}
        self._ld = {}
        self._st = {}
        self._sd = {}

    def ld(self, chain):
        if chain not in self._ld:
            assert len(self._ld) < 16, "out of ld lanes"
            self._ld[chain] = f"ld{len(self._ld)}"
        return self._ld[chain]

    def st(self, chain):
        if chain not in self._st:
            assert len(self._st) < 8, "out of st lanes"
            self._st[chain] = f"st{len(self._st)}"
        return self._st[chain]

    def sd(self, chain):
        if chain not in self._sd:
            assert len(self._sd) < 4, "out of sl lanes"
            self._sd[chain] = f"sl{len(self._sd)}"
        return self._sd[chain]

    def op(self, eng, fn, deps=(), inc=None, n=1, pre_nops=0, pre_drain=False):
        p = self.p
        waits = {}
        for t in deps:
            if t is None:
                continue
            sem, val, ep, src = t
            if ep < p.epoch:
                continue
            if val > waits.get(sem, -1):
                waits[sem] = val
        wl = []
        for sem, val in waits.items():
            key = (eng, sem)
            if p.waited.get(key, -1) >= val:
                continue
            p.waited[key] = val
            wl.append((sem, val))
        tok = None
        if inc is not None:
            p.C[inc] += n
            tok = (inc, p.C[inc], p.epoch, eng)
        self.q[eng].append((fn, wl, inc, n, pre_nops, pre_drain))
        return tok

    def run(self):
        p = self.p
        nc = p.nc
        with nc.Block(self.name) as blk:
            for eng_name in ENGINES:
                ops = self.q[eng_name]
                if not ops:
                    continue

                def body(e, ops=ops, eng_name=eng_name):
                    for fn, wl, inc, n, pre_nops, pre_drain in ops:
                        for sem, val in wl:
                            e.wait_ge(p.sems[sem], val)
                        if pre_drain:
                            e.drain()
                        inst = fn(e)
                        if inc is not None:
                            inst.then_inc(p.sems[inc], n)

                getattr(blk, eng_name)(body)
        p.epoch += 1


def build_program(debug=False, stages=("prep", "a1", "a2", "norm", "b", "c")):
    nc = bass.Bass("TRN2", target_bir_lowering=False)

    x_d = nc.dram_tensor("L0_attn_x", [NSEQ, T, D], F32, kind="ExternalInput")
    g_d = nc.dram_tensor("L0_attn_ln_g", [D], F32, kind="ExternalInput")
    b_d = nc.dram_tensor("L0_attn_ln_b", [D], F32, kind="ExternalInput")
    Wq_d = nc.dram_tensor("L0_attn_Wq", [D, D], F32, kind="ExternalInput")
    bq_d = nc.dram_tensor("L0_attn_bq", [D], F32, kind="ExternalInput")
    Wk_d = nc.dram_tensor("L0_attn_Wk", [D, D], F32, kind="ExternalInput")
    bk_d = nc.dram_tensor("L0_attn_bk", [D], F32, kind="ExternalInput")
    Wv_d = nc.dram_tensor("L0_attn_Wv", [D, D], F32, kind="ExternalInput")
    bv_d = nc.dram_tensor("L0_attn_bv", [D], F32, kind="ExternalInput")
    wg_d = nc.dram_tensor("L0_ffn_w_gate", [HID, D], I32, kind="ExternalInput")
    wu_d = nc.dram_tensor("L0_ffn_w_up", [HID, D], I32, kind="ExternalInput")
    wd_d = nc.dram_tensor("L0_ffn_w_down", [D, HID], I32, kind="ExternalInput")
    out_d = nc.dram_tensor("out", [NSEQ, T, D], F32, kind="ExternalOutput")

    sk = "ExternalOutput" if debug else "Internal"
    wg16 = nc.dram_tensor("wg16", [HID, D], BF16, kind="Internal")
    wu16 = nc.dram_tensor("wu16", [HID, D], BF16, kind="Internal")
    wd16 = nc.dram_tensor("wd16", [D, HID], BF16, kind="Internal")
    wgT = nc.dram_tensor("wgT", [12 * 8 * 128 * 256], BF16, kind="Internal")
    wuT = nc.dram_tensor("wuT", [12 * 8 * 128 * 256], BF16, kind="Internal")
    wdT = nc.dram_tensor("wdT", [HID, D], BF16, kind="Internal")
    x1s = nc.dram_tensor("x1s", [T, D], F32, kind=sk)
    dbg_sm = nc.dram_tensor("dbg_sm", [128, 64], F32, kind=sk)
    x2s = nc.dram_tensor("x2s", [T, D], F32, kind=sk)

    es = ExitStack()
    sems = {}
    for s in (["mm", "ev", "sc", "gp"] + [f"ld{i}" for i in range(16)]
              + [f"st{i}" for i in range(8)] + [f"sl{i}" for i in range(4)]):
        sems[s] = es.enter_context(nc.semaphore(s))
    P = Prog(nc, sems)

    SB = lambda name, shape, dt: es.enter_context(nc.sbuf_tensor(name, shape, dt))
    ident = SB("ident", [128, 128], BF16)
    id16 = SB("id16", [16, 16], F32)
    ones_r = SB("ones_r", [1, 512], BF16)
    selp = SB("selp", [128, 128], BF16)
    gch = SB("gch", [128, 8], F32)
    bch = SB("bch", [128, 8], F32)
    bchb = SB("bchb", [128, 8], BF16)
    cb = SB("cb", [128, 16], F32)
    cb8 = SB("cb8", [128, 8], F32)
    crow = SB("crow", [1, 16], BF16)
    crow2 = SB("crow2", [1, 8], BF16)
    n16 = SB("n16", [16, 8], F32)
    sm_s = SB("sm_s", [128, 16], F32)
    sm_q = SB("sm_q", [128, 16], F32)
    sm_m = SB("sm_m", [128, 16], F32)
    sm_r = SB("sm_r", [128, 16], F32)
    sm_t1 = SB("sm_t1", [128, 16], F32)
    sm_t2 = SB("sm_t2", [128, 16], F32)
    eps_c = SB("eps_c", [128, 1], F32)
    ones_f = SB("ones_f", [1, 1], F32)

    PS0 = es.enter_context(nc.psum_tensor("PS0", [128, 2048], F32))
    PS1 = es.enter_context(nc.psum_tensor("PS1", [128, 2048], F32))

    def flush(ph):
        for lane in list(ph._ld.values()):
            ph.op("sync", lambda e, l=lane: e.wait_ge(sems[l], P.C[l]))
        for lane in list(ph._st.values()):
            ph.op("gpsimd", lambda e, l=lane: e.wait_ge(sems[l], P.C[l]))
        for lane in list(ph._sd.values()):
            ph.op("scalar", lambda e, l=lane: e.wait_ge(sems[l], P.C[l]))

    def stats_chain(ph, b, t_red, t_sq, prev_done):
        c0, c1 = b * 4, b * 4 + 4
        deps = [t_red[b * 4 + i] for i in range(4)]
        t_m = ph.op("scalar", lambda e: e.mul(sm_m[:, c0:c1], sm_s[:, c0:c1], 1.0 / D),
                    deps=deps, inc="sc")
        t_m2 = ph.op("scalar", lambda e: e.activation(
            sm_t1[:, c0:c1], sm_m[:, c0:c1], AF.Square),
            deps=[prev_done, t_m], inc="sc")
        t_var = ph.op("vector", lambda e: e.scalar_tensor_tensor(
            out=sm_t2[:, c0:c1], in0=sm_q[:, c0:c1], scalar=1.0 / D,
            in1=sm_t1[:, c0:c1], op0=ALU.mult, op1=ALU.subtract),
            deps=[t_m2] + [t_sq[b * 4 + i] for i in range(4)], inc="ev")
        t_sd = ph.op("scalar", lambda e: e.activation(
            sm_t1[:, c0:c1], sm_t2[:, c0:c1], AF.Sqrt, bias=eps_c[:, 0:1]),
            deps=[t_var], inc="sc")
        return ph.op("vector", lambda e: e.reciprocal(
            sm_r[:, c0:c1], sm_t1[:, c0:c1]), deps=[t_sd], inc="ev")

    def prep():
        ph = Phase(P, "prep_const")

        def mset(buf, val):
            return ph.op("gpsimd", lambda e, buf=buf, val=val: e.memset(buf, val),
                         inc="gp")

        def asel(buf, dep, cmp, fill, base, pat, cm):
            return ph.op("gpsimd", lambda e, buf=buf: e.affine_select(
                out=buf, in_=buf, compare_op=cmp, fill=fill, base=base,
                pattern=pat, channel_multiplier=cm), deps=[dep], inc="gp")

        t0 = mset(ident[:], 0.0)
        asel(ident[:], t0, ALU.not_equal, 1.0, 0, [[-1, 128]], 1)
        t0 = mset(id16[:], 0.0)
        asel(id16[:], t0, ALU.not_equal, 1.0, 0, [[-1, 16]], 1)
        mset(ones_r[:], 1.0)
        mset(eps_c[:], EPS)
        t_of = mset(ones_f[:], 1.0)
        for dc in range(8):
            sl_ = selp[:, dc * 16:(dc + 1) * 16]
            t0 = mset(sl_, 1.0)
            t1 = asel(sl_, t0, ALU.is_ge, 0.0, 128 * dc, [[-64, 16]], 1)
            asel(sl_, t1, ALU.is_ge, 0.0, 63 - 128 * dc, [[64, 16]], -1)
        ph.run()

        ph = Phase(P, "prep_gb")
        with nc.sbuf_tensor("pgb_row", [1, 2048], F32) as grow:
            t_g = ph.op("sync", lambda e: e.dma_start(
                grow[0:1, 0:1024], g_d.ap().rearrange("(a d) -> a d", a=1)),
                inc=ph.ld("ga"), n=16)
            t_b = ph.op("sync", lambda e: e.dma_start(
                grow[0:1, 1024:2048], b_d.ap().rearrange("(a d) -> a d", a=1)),
                inc=ph.ld("gb"), n=16)

            def gmm(e):
                for dc in range(16):
                    inst = e.matmul(PS0[:, dc:dc + 1],
                                    grow[0:1, dc * 128:(dc + 1) * 128],
                                    ones_f[0:1, 0:1], start=True, stop=True)
                return inst

            t_mm = ph.op("tensor", gmm, deps=[t_g, t_b, t_of], inc="mm")
            ph.op("vector", lambda e: e.tensor_copy(gch[:], PS0[:, 0:8]),
                  deps=[t_mm], inc="ev")
            t_bc2 = ph.op("vector", lambda e: e.tensor_copy(bch[:], PS0[:, 8:16]),
                          deps=[t_mm], inc="ev")
            ph.op("scalar", lambda e: e.copy(bchb[:], bch[:]),
                  deps=[t_bc2], inc="sc")
            ph.run()

    def emit_prep(ph, wstg, wbf, tpb, nslot=1):
        conv = {}
        stor = {}
        lastw = {}
        ci = 0
        for (wsrc, wdst, nch, cw) in ((wg_d, wg16, 24, D), (wu_d, wu16, 24, D),
                                      (wd_d, wd16, 8, HID)):
            for ch in range(nch):
                bdep2 = None
                if cw == D:
                    off = (ci % 2) * 1024
                    bdep = stor.get(ci - 2)
                else:
                    off = 0
                    bdep = stor.get(ci - 1)
                    bdep2 = stor.get(ci - 2)
                soff = (ci % nslot) * HID
                t_l = ph.op("gpsimd", lambda e, s=wsrc, c=ch, cw=cw, soff=soff: e.dma_start(
                    wstg[:, soff:soff + cw],
                    s.ap()[c * 128:(c + 1) * 128, :]),
                    deps=[conv.get(ci - nslot)], inc=ph.st(f"pw{ci % nslot}"), n=16)
                conv[ci] = ph.op("vector", lambda e, cw=cw, off=off, soff=soff: e.tensor_scalar_mul(
                    wbf[:, off: off + cw], wstg[:, soff:soff + cw], QS),
                    deps=[t_l, bdep, bdep2], inc="ev")
                stor[ci] = ph.op("gpsimd", lambda e, d=wdst, c=ch, cw=cw, off=off: e.dma_start(
                    d.ap()[c * 128:(c + 1) * 128, :], wbf[:, off: off + cw]),
                    deps=[conv[ci]], inc=ph.st(f"pb{(ci % 2) if cw == D else 0}"), n=16)
                ci += 1
            lastw[len(lastw)] = [stor[ci - 1], stor.get(ci - 2)]
        tlast = {}
        ready = []
        ti_ = 0
        for wi, (wsrc16, wdstT) in enumerate(((wg16, wgT), (wu16, wuT))):
            for dc in range(8):
                sl = ti_ % 2
                t_t = ph.op("scalar", lambda e, s=wsrc16, dc=dc, sl=sl: e.dma_start(
                    tpb[:, sl * HID:(sl + 1) * HID],
                    s.ap()[:, dc * 128:(dc + 1) * 128], transpose=True),
                    deps=[tlast.get(sl)] + lastw[wi], inc=ph.sd(f"pt{sl}"), n=16)
                tlast[sl] = ph.op("gpsimd", lambda e, d=wdstT, dc=dc, sl=sl: e.dma_start(
                    d.ap().rearrange("(a b p h) -> b p a h", a=12, b=8, p=128)[dc],
                    tpb[:, sl * HID:(sl + 1) * HID].rearrange("p (a h) -> p a h", a=12)),
                    deps=[t_t], inc=ph.st(f"pp{sl}"), n=16)
                ti_ += 1
            ready.append([tlast.get(0), tlast.get(1)])
        for hc in range(24):
            sl = ti_ % 2
            t_t = ph.op("scalar", lambda e, hc=hc, sl=sl: e.dma_start(
                tpb[:, sl * HID: sl * HID + D],
                wd16.ap()[:, hc * 128:(hc + 1) * 128], transpose=True),
                deps=[tlast.get(sl)] + lastw[2], inc=ph.sd(f"pt{sl}"), n=16)
            tlast[sl] = ph.op("gpsimd", lambda e, hc=hc, sl=sl: e.dma_start(
                wdT.ap()[hc * 128:(hc + 1) * 128, :],
                tpb[:, sl * HID: sl * HID + D]),
                deps=[t_t], inc=ph.st(f"pp{sl}"), n=16)
            ti_ += 1
        ready.append([tlast.get(0), tlast.get(1)])
        return ready

    def ffn_phase(name, src_ap, dst_ap, with_prep=False, wprep2=None):
        ph = Phase(P, name)
        wready = [[], [], []]
        pes = ExitStack()
        if with_prep:
            wstg = pes.enter_context(nc.sbuf_tensor(name + "_pws", [128, HID], I32))
            wbf = pes.enter_context(nc.sbuf_tensor(name + "_pwb", [128, HID + 1024], BF16))
            tpb = pes.enter_context(nc.sbuf_tensor(name + "_pwt", [128, 2 * HID], BF16))

        nslots = 8
        with (
            nc.sbuf_tensor(name + "_xst", [128, nslots * 1024], F32) as xst,
            nc.sbuf_tensor(name + "_xntm", [128, 2 * 1024], BF16) as xntm,
            nc.sbuf_tensor(name + "_xnfm", [128, 8 * 1024], BF16) as xnfm,
            nc.sbuf_tensor(name + "_hsb", [128, 24 * 1024], BF16) as hsb,
            nc.sbuf_tensor(name + "_wslg", [128, 2 * 2048], BF16) as wslg,
            nc.sbuf_tensor(name + "_wslu", [128, 2 * 2048], BF16) as wslu,
            nc.sbuf_tensor(name + "_wdsb", [128, 24 * 1024], BF16) as wdsb,
            nc.sbuf_tensor(name + "_sq", [128, 1024], BF16) as sqscr,
            nc.sbuf_tensor(name + "_sg", [128, 2 * 512], BF16) as sgb,
            nc.sbuf_tensor(name + "_xo", [128, (1 if with_prep else 2) * 1024], F32) as x1o,
        ):
            if with_prep:
                wready = emit_prep(ph, wstg, wbf, tpb)
            t_wd_holder = [None]

            t_load = [None] * NT
            t_xn = [None] * NT
            t_ev = [None] * NT
            t_sq = [None] * NT
            t_red = [None] * NT
            t_tp = [None] * NT
            t_statsdone = [None] * 4
            t_evac_bank = [None, None]
            t_gu_mm = {}
            t_silu = {}
            t_hmul = {}
            t_down = {}
            t_stt = {}
            t_store = [None] * NT
            t_slab = {}
            gu_last = [None, None]

            for g in range(2):
                for ti in range(8):
                    tt = g * 8 + ti
                    slot = tt % nslots
                    t_load[tt] = ph.op("sync", lambda e, tt=tt, slot=slot: e.dma_start(
                        xst[:, slot * 1024:(slot + 1) * 1024],
                        src_ap[tt * 128:(tt + 1) * 128, :]),
                        deps=([t_xn[tt - nslots], t_sq[tt - nslots],
                               t_stt.get(((tt - nslots) // 8, (tt - nslots) % 8, 1))]
                              if tt >= nslots else []),
                        inc=ph.ld(f"x{slot}"), n=16)
                    t_red[tt] = ph.op("vector", lambda e, tt=tt, slot=slot: e.reduce_sum(
                        sm_s[:, tt:tt + 1], xst[:, slot * 1024:(slot + 1) * 1024],
                        axis=AX.X), deps=[t_load[tt]], inc="ev")
                    t_sq[tt] = ph.op("scalar", lambda e, tt=tt, slot=slot: e.activation(
                        sqscr[:], xst[:, slot * 1024:(slot + 1) * 1024], AF.Square,
                        accum_out=sm_q[:, tt:tt + 1]),
                        deps=[t_load[tt], t_sq[tt - 1] if tt >= 1 else None],
                        inc="sc")
                    if tt % 4 == 3:
                        b_ = tt // 4
                        t_statsdone[b_] = stats_chain(
                            ph, b_, t_red, t_sq,
                            t_statsdone[b_ - 1] if b_ >= 1 else None)

                for ti in range(8):
                    tt = g * 8 + ti
                    slot = tt % nslots
                    t_xn[tt] = ph.op("vector", lambda e, tt=tt, slot=slot: e.tensor_scalar(
                        out=xntm[:, (tt % 2) * 1024:(tt % 2 + 1) * 1024],
                        in0=xst[:, slot * 1024:(slot + 1) * 1024],
                        scalar1=sm_m[:, tt:tt + 1], scalar2=sm_r[:, tt:tt + 1],
                        op0=ALU.subtract, op1=ALU.mult),
                        deps=[t_statsdone[tt // 4],
                              t_tp[tt - 2] if tt >= 2 else None],
                        inc="ev")
                    bank = tt % 2
                    tview = PS0[:, bank * 512:(bank + 1) * 512].bitcast(BF16)

                    def tp_fn(e, tt=tt, tview=tview):
                        for dc in range(8):
                            inst = e.transpose(
                                tview[:, dc * 128:(dc + 1) * 128],
                                xntm[:, (tt % 2) * 1024 + dc * 128:
                                     (tt % 2) * 1024 + (dc + 1) * 128],
                                ident[:])
                        return inst

                    t_tp[tt] = ph.op("tensor", tp_fn,
                                     deps=[t_xn[tt], t_evac_bank[bank]], inc="mm")
                    t_ev[tt] = ph.op("vector", lambda e, tview=tview, ti=ti: e.tensor_copy(
                        xnfm[:].rearrange("p (a t) -> p a t", a=8)[:, :, ti * 128:(ti + 1) * 128],
                        tview[:].rearrange("p (a t) -> p a t", a=8)),
                        deps=[t_tp[tt], gu_last[g - 1] if g >= 1 else None],
                        inc="ev")
                    t_evac_bank[bank] = t_ev[tt]

                grp_ev_h = {0: t_ev[g * 8 + 3], 1: t_ev[g * 8 + 7]}

                for hc2 in range(12):
                    sl = hc2 % 2
                    if g == 0 and hc2 < 2:
                        slab_dep = None
                    elif hc2 >= 2:
                        slab_dep = t_gu_mm.get((g, hc2 - 2, 1, 1))
                    else:
                        slab_dep = t_gu_mm.get((g - 1, 10 + hc2, 1, 1))
                    t_lg = ph.op("sync", lambda e, hc2=hc2, sl=sl: e.dma_start(
                        wslg[:, sl * 2048:(sl + 1) * 2048].rearrange("p (b h) -> p b h", b=8),
                        wgT.ap().rearrange("(a b p h) -> a p b h", a=12, b=8, p=128)[hc2]),
                        deps=[slab_dep] + wready[0], inc=ph.ld(f"g{sl}"), n=16)
                    t_lu = ph.op("sync", lambda e, hc2=hc2, sl=sl: e.dma_start(
                        wslu[:, sl * 2048:(sl + 1) * 2048].rearrange("p (b h) -> p b h", b=8),
                        wuT.ap().rearrange("(a b p h) -> a p b h", a=12, b=8, p=128)[hc2]),
                        deps=[slab_dep] + wready[1], inc=ph.ld(f"u{sl}"), n=16)
                    for hcl in range(2):
                        hc = hc2 * 2 + hcl
                        for th in range(2):
                            k = hc * 2 + th
                            par = k % 2
                            gview = PS0[:, 1024:1536] if par == 0 else PS1[:, 0:512]
                            uview = PS0[:, 1536:2048] if par == 0 else PS1[:, 512:1024]

                            def gu_fn(e, hcl=hcl, th=th, sl=sl, gview=gview, uview=uview):
                                for dc in range(8):
                                    lg = wslg[:, sl * 2048 + dc * 256 + hcl * 128:
                                              sl * 2048 + dc * 256 + (hcl + 1) * 128]
                                    rhs = xnfm[:, dc * 1024 + th * 512:
                                               dc * 1024 + (th + 1) * 512]
                                    e.matmul(gview, lg, rhs, start=(dc == 0), stop=(dc == 7))
                                for dc in range(8):
                                    lu = wslu[:, sl * 2048 + dc * 256 + hcl * 128:
                                              sl * 2048 + dc * 256 + (hcl + 1) * 128]
                                    rhs = xnfm[:, dc * 1024 + th * 512:
                                               dc * 1024 + (th + 1) * 512]
                                    inst = e.matmul(uview, lu, rhs, start=(dc == 0), stop=(dc == 7))
                                return inst

                            t_mmgu = ph.op("tensor", gu_fn,
                                           deps=[t_lg, t_lu, grp_ev_h[th],
                                                 t_silu.get((g, k - 2)),
                                                 t_hmul.get((g, k - 2))],
                                           inc="mm")
                            t_gu_mm[(g, hc2, hcl, th)] = t_mmgu
                            gu_last[g] = t_mmgu
                            t_silu[(g, k)] = ph.op(
                                "scalar", lambda e, gview=gview, th=th: e.activation(
                                    sgb[:, (th % 2) * 512:(th % 2 + 1) * 512],
                                    gview, AF.Silu),
                                deps=[t_mmgu, t_hmul.get((g, k - 2)),
                                      t_silu.get((g, k - 2))], inc="sc")
                            t_hmul[(g, k)] = ph.op(
                                "vector", lambda e, uview=uview, hc=hc, th=th: e.tensor_mul(
                                    hsb[:, hc * 1024 + th * 512: hc * 1024 + (th + 1) * 512],
                                    uview, sgb[:, (th % 2) * 512:(th % 2 + 1) * 512]),
                                deps=[t_silu[(g, k)], t_mmgu,
                                      t_down.get((g - 1, 7, 1))], inc="ev")

                hmul_done = t_hmul[(g, 47)]
                if g == 0:
                    t_wd_holder[0] = ph.op("sync", lambda e: e.dma_start(
                        wdsb[:].rearrange("p (a h) -> p a h", a=24),
                        wdT.ap().rearrange("(a p) h -> p a h", p=128)),
                        deps=wready[2], inc=ph.ld("wd"), n=16)
                t_wd = t_wd_holder[0]

                for ti in range(8):
                    tt = g * 8 + ti
                    for oh in range(2):
                        dview = PS1[:, 1024 + oh * 512: 1536 + oh * 512]

                        def dn_fn(e, ti=ti, oh=oh, dview=dview):
                            for hc in range(24):
                                inst = e.matmul(
                                    dview,
                                    hsb[:, hc * 1024 + ti * 128: hc * 1024 + (ti + 1) * 128],
                                    wdsb[:, hc * 1024 + oh * 512: hc * 1024 + (oh + 1) * 512],
                                    start=(hc == 0), stop=(hc == 23))
                            return inst

                        t_down[(g, ti, oh)] = ph.op(
                            "tensor", dn_fn,
                            deps=[hmul_done, t_wd if (g == 0) else None,
                                  t_stt.get((g, ti - 1, oh)),
                                  t_stt.get((g - 1, 7, oh))],
                            inc="mm")
                        xom = 1 if with_prep else 2
                        t_stt[(g, ti, oh)] = ph.op(
                            "vector", lambda e, ti=ti, oh=oh, dview=dview, xom=xom: e.scalar_tensor_tensor(
                                out=x1o[:, (ti % xom) * 1024 + oh * 512:
                                        (ti % xom) * 1024 + (oh + 1) * 512],
                                in0=dview, scalar=0.5,
                                in1=xst[:, ((g * 8 + ti) % nslots) * 1024 + oh * 512:
                                        ((g * 8 + ti) % nslots) * 1024 + (oh + 1) * 512],
                                op0=ALU.mult, op1=ALU.add),
                            deps=[t_down[(g, ti, oh)],
                                  t_store[tt - xom] if tt >= xom else None],
                            inc="ev")
                    t_store[tt] = ph.op("gpsimd", lambda e, tt=tt, ti=ti, xom=(1 if with_prep else 2): e.dma_start(
                        dst_ap[tt * 128:(tt + 1) * 128, :],
                        x1o[:, (ti % xom) * 1024:(ti % xom + 1) * 1024]),
                        deps=[t_stt[(g, ti, 0)], t_stt[(g, ti, 1)]],
                        inc=ph.st(f"o{ti % 2}"), n=16)

            if wprep2 is not None:
                wq_s, bbq = wprep2
                wcast = {}
                for dc in range(8):
                    wcast[dc] = ph.op(
                        "gpsimd", lambda e, dc=dc: e.dma_start(
                            wq_s[:, dc * 1024:(dc + 1) * 1024],
                            Wq_d.ap()[dc * 128:(dc + 1) * 128, :]),
                        inc=ph.st(f"w{dc % 2}"), n=16)
                t_lb = ph.op("gpsimd", lambda e: e.dma_start(
                    bbq[0:1, :], bq_d.ap().rearrange("(a d) -> a d", a=1)),
                    inc=ph.st("bv"), n=16)

                def bmm(e):
                    for oi in range(2):
                        for dc in range(8):
                            inst = e.matmul(
                                PS0[0:1, oi * 512:(oi + 1) * 512],
                                bchb[:, dc:dc + 1],
                                wq_s[:, dc * 1024 + oi * 512:
                                     dc * 1024 + (oi + 1) * 512],
                                start=(dc == 0), stop=(dc == 7))
                    return inst

                t_bm = ph.op("tensor", bmm,
                             deps=[wcast[dc] for dc in range(8)]
                             + [t_ev[14], t_ev[15]],
                             inc="mm")
                ph.op("vector", lambda e: e.tensor_add(
                    bbq[0:1, :], PS0[0:1, 0:1024], bbq[0:1, :]),
                    deps=[t_bm, t_lb], inc="ev")
                for dc in range(8):
                    ph.op("vector", lambda e, dc=dc: e.tensor_scalar_mul(
                        wq_s[:, dc * 1024:(dc + 1) * 1024],
                        wq_s[:, dc * 1024:(dc + 1) * 1024],
                        gch[:, dc:dc + 1]),
                        deps=[t_bm], inc="ev", pre_nops=1)
            flush(ph)
            ph.run()
        pes.close()

    def qkv_phase(seq, K_sb, Q_sb, V_sb, Wqs, bbq):
        nm = f"a2s{seq}"
        ph = Phase(P, nm)
        with (
            nc.sbuf_tensor(nm + "_Wk", [128, 8 * 1024], BF16) as Wks,
            nc.sbuf_tensor(nm + "_Wv", [128, 8 * 1024], BF16) as Wvs,
            nc.sbuf_tensor(nm + "_bvec", [1, 1024], F32) as bvec,
            nc.sbuf_tensor(nm + "_bb16", [1, 2 * 1024], BF16) as bb16,
            nc.sbuf_tensor(nm + "_xst", [128, 6 * 1024], F32) as xst2,
            nc.sbuf_tensor(nm + "_xntm", [128, 2 * 1024], BF16) as xntm2,
            nc.sbuf_tensor(nm + "_xnfm", [128, 2 * 8 * 512], BF16) as xnfm2,
            nc.sbuf_tensor(nm + "_sq", [128, 1024], BF16) as sqscr2,
        ):
            cast = {}
            for wi, wsrc, wdst in ((1, Wk_d, Wks), (2, Wv_d, Wvs)):
                for dc in range(8):
                    cast[(wi, dc)] = ph.op(
                        "gpsimd", lambda e, s_=wsrc, d_=wdst, dc=dc: e.dma_start(
                            d_[:, dc * 1024:(dc + 1) * 1024],
                            s_.ap()[dc * 128:(dc + 1) * 128, :]),
                        inc=ph.st(f"c{wi}{dc % 2}"), n=16)
            t_fold = {}
            t_ba_prev = None
            for wi, wsb, bsrc in ((1, Wks, bk_d), (2, Wvs, bv_d)):
                t_lb = ph.op("sync", lambda e, s_=bsrc: e.dma_start(
                    bvec[0:1, :], s_.ap().rearrange("(a d) -> a d", a=1)),
                    deps=[t_ba_prev], inc=ph.ld("bv"), n=16)

                def bmm(e, wsb=wsb):
                    for oi in range(2):
                        for dc in range(8):
                            inst = e.matmul(
                                PS1[0:1, oi * 512:(oi + 1) * 512],
                                bchb[:, dc:dc + 1],
                                wsb[:, dc * 1024 + oi * 512:
                                    dc * 1024 + (oi + 1) * 512],
                                start=(dc == 0), stop=(dc == 7))
                    return inst

                t_bm = ph.op("tensor", bmm,
                             deps=[cast[(wi, dc)] for dc in range(8)]
                             + [t_ba_prev], inc="mm")
                t_ba = ph.op("vector", lambda e, wi=wi: e.tensor_add(
                    bb16[0:1, (wi - 1) * 1024:wi * 1024],
                    PS1[0:1, 0:1024], bvec[0:1, :]),
                    deps=[t_bm, t_lb], inc="ev")
                t_ba_prev = t_ba
                t_fold[("b", wi)] = t_ba
                for dc in range(8):
                    t_fold[(wi, dc)] = ph.op(
                        "vector", lambda e, wsb=wsb, dc=dc: e.tensor_scalar_mul(
                            wsb[:, dc * 1024:(dc + 1) * 1024],
                            wsb[:, dc * 1024:(dc + 1) * 1024],
                            gch[:, dc:dc + 1]),
                        deps=[t_bm], inc="ev", pre_nops=1)
            t_bias_done = None

            def vones(e):
                for t in range(16):
                    inst = e.memset(
                        V_sb[:, t * 1088:(t + 1) * 1088].rearrange(
                            "p (h c) -> p h c", h=16)[:, :, 64:65], 1.0)
                return inst
            t_vones = ph.op("gpsimd", vones, inc="gp")

            t_load = [None] * NT
            t_red = [None] * NT
            t_sq = [None] * NT
            t_xn = [None] * NT
            t_tp = [None] * NT
            t_ev = [None] * NT
            t_statsdone = [None] * 4
            t_evac_bank = [None, None]
            t_qk = {}
            t_qke = {}
            t_v = {}
            t_ve = {}

            def emit_stats_a2(g4):
                for ti in range(4):
                    tt = g4 * 4 + ti
                    slot = tt % 6
                    t_load[tt] = ph.op("sync", lambda e, tt=tt, slot=slot: e.dma_start(
                        xst2[:, slot * 1024:(slot + 1) * 1024],
                        x1s.ap()[tt * 128:(tt + 1) * 128, :]),
                        deps=[t_xn[tt - 6], t_sq[tt - 6]] if tt >= 6 else [],
                        inc=ph.ld(f"x{slot}"), n=16)
                    t_red[tt] = ph.op("vector", lambda e, tt=tt, slot=slot: e.reduce_sum(
                        sm_s[:, tt:tt + 1], xst2[:, slot * 1024:(slot + 1) * 1024],
                        axis=AX.X), deps=[t_load[tt]], inc="ev")
                    t_sq[tt] = ph.op("scalar", lambda e, tt=tt, slot=slot: e.activation(
                        sqscr2[:], xst2[:, slot * 1024:(slot + 1) * 1024], AF.Square,
                        accum_out=sm_q[:, tt:tt + 1]),
                        deps=[t_load[tt], t_sq[tt - 1] if tt >= 1 else None],
                        inc="sc")
                    if tt % 4 == 3:
                        b_ = tt // 4
                        t_statsdone[b_] = stats_chain(
                            ph, b_, t_red, t_sq,
                            t_statsdone[b_ - 1] if b_ >= 1 else None)

            emit_stats_a2(0)
            for g4 in range(4):
                for ti in range(4):
                    tt = g4 * 4 + ti
                    slot = tt % 6
                    t_xn[tt] = ph.op("vector", lambda e, tt=tt, slot=slot: e.tensor_scalar(
                        out=xntm2[:, (tt % 2) * 1024:(tt % 2 + 1) * 1024],
                        in0=xst2[:, slot * 1024:(slot + 1) * 1024],
                        scalar1=sm_m[:, tt:tt + 1], scalar2=sm_r[:, tt:tt + 1],
                        op0=ALU.subtract, op1=ALU.mult),
                        deps=[t_statsdone[tt // 4],
                              t_tp[tt - 2] if tt >= 2 else None],
                        inc="ev")
                    bank = tt % 2
                    tview = PS0[:, bank * 512:(bank + 1) * 512].bitcast(BF16)

                    def tp_fn(e, tt=tt, tview=tview):
                        for dc in range(8):
                            inst = e.transpose(
                                tview[:, dc * 128:(dc + 1) * 128],
                                xntm2[:, (tt % 2) * 1024 + dc * 128:
                                      (tt % 2) * 1024 + (dc + 1) * 128],
                                ident[:])
                        return inst

                    t_tp[tt] = ph.op("tensor", tp_fn,
                                     deps=[t_xn[tt], t_evac_bank[bank],
                                           t_bias_done if tt <= 1 else None],
                                     inc="mm")
                    gp_ = (tt // 4) % 2
                    t_ev[tt] = ph.op("vector", lambda e, tview=tview, ti=ti, gp_=gp_: e.tensor_copy(
                        xnfm2[:, gp_ * 4096:(gp_ + 1) * 4096].rearrange(
                            "p (a t) -> p a t", a=8)[:, :, ti * 128:(ti + 1) * 128],
                        tview[:].rearrange("p (a t) -> p a t", a=8)),
                        deps=[t_tp[tt], t_v.get((g4 - 2, 3, 1))], inc="ev")
                    t_evac_bank[tt % 2] = t_ev[tt]

                if g4 < 3:
                    emit_stats_a2(g4 + 1)
                grp_ev = t_ev[g4 * 4 + 3]

                for pi, (wsb, dst_sb) in enumerate(((Wqs, Q_sb), (Wks, K_sb))):
                    for oc in range(8):
                        par = oc % 2
                        qview = PS0[:, 1024 + par * 512: 1536 + par * 512]

                        gp_ = g4 % 2

                        brow_ = bbq if pi == 0 else bb16

                        def qk_fn(e, wsb=wsb, oc=oc, qview=qview, gp_=gp_,
                                  brow_=brow_):
                            for dc in range(8):
                                e.matmul(qview,
                                         wsb[:, dc * 1024 + oc * 128:
                                             dc * 1024 + (oc + 1) * 128],
                                         xnfm2[:, gp_ * 4096 + dc * 512:
                                               gp_ * 4096 + (dc + 1) * 512],
                                         start=(dc == 0), stop=False)
                            inst = e.matmul(qview,
                                            brow_[0:1, oc * 128:(oc + 1) * 128],
                                            ones_r[0:1, 0:512], start=False, stop=True)
                            return inst

                        if pi == 1 and oc < 2:
                            bank_dep = t_qke.get((g4, 0, 6 + par))
                        else:
                            bank_dep = t_qke.get((g4, pi, oc - 2))
                        if pi == 0 and oc < 2:
                            bank_dep = t_qke.get((g4 - 1, 1, 6 + par))
                        kdeps = ([t_fold[(1, dc)] for dc in range(8)]
                                 + [t_fold[("b", 1)]]
                                 if (g4 == 0 and pi == 1 and oc == 0) else [])
                        t_qk[(g4, pi, oc)] = ph.op(
                            "tensor", qk_fn,
                            deps=[grp_ev, bank_dep] + kdeps,
                            inc="mm")
                        t_qke[(g4, pi, oc)] = ph.op(
                            "vector", lambda e, d=dst_sb, oc=oc, g4=g4, qview=qview: e.tensor_copy(
                                d[:, oc * 2048 + g4 * 512: oc * 2048 + (g4 + 1) * 512],
                                qview),
                            deps=[t_qk[(g4, pi, oc)]], inc="ev")

                for ti in range(4):
                    tt = g4 * 4 + ti
                    for ov in range(2):
                        vview = PS1[:, ov * 512:(ov + 1) * 512]
                        if ti == 0:
                            vdep = t_ve.get((g4 - 1, 3, ov))
                        else:
                            vdep = t_ve.get((g4, ti - 1, ov))

                        gp_ = g4 % 2

                        def v_fn(e, ti=ti, ov=ov, vview=vview, gp_=gp_):
                            for dc in range(8):
                                e.matmul(vview,
                                         xnfm2[:, gp_ * 4096 + dc * 512 + ti * 128:
                                               gp_ * 4096 + dc * 512 + (ti + 1) * 128],
                                         Wvs[:, dc * 1024 + ov * 512:
                                             dc * 1024 + (ov + 1) * 512],
                                         start=(dc == 0), stop=False)
                            inst = e.matmul(vview, ones_r[0:1, 0:128],
                                            bb16[0:1, 1024 + ov * 512:
                                                 1024 + (ov + 1) * 512],
                                            start=False, stop=True)
                            return inst

                        vdeps = ([t_fold[(2, dc)] for dc in range(8)]
                                 + [t_fold[("b", 1)], t_fold[("b", 2)]]
                                 if (g4 == 0 and ti == 0 and ov == 0) else [])
                        t_v[(g4, ti, ov)] = ph.op(
                            "tensor", v_fn, deps=[grp_ev, vdep] + vdeps, inc="mm")
                        t_ve[(g4, ti, ov)] = ph.op(
                            "vector", lambda e, tt=tt, ov=ov, vview=vview: e.tensor_copy(
                                V_sb[:, tt * 1088 + ov * 8 * 68:
                                     tt * 1088 + (ov * 8 + 8) * 68].rearrange(
                                         "p (h c) -> p h c", h=8)[:, :, 0:64],
                                vview.rearrange("p (h c) -> p h c", h=8)),
                            deps=[t_v[(g4, ti, ov)], t_vones], inc="ev")

            flush(ph)
            ph.run()

    def emit_norms(ph, K_sb, Q_sb, nsq):
        t_red = {}
        for pi, src_sb in enumerate((Q_sb, K_sb)):
            t_sq = {}
            t_mm = {}
            for dc in range(8):
                sl = dc % 2
                t_sq[dc] = ph.op("vector", lambda e, s=src_sb, dc=dc, sl=sl: e.tensor_mul(
                    nsq[:, sl * 2048:(sl + 1) * 2048],
                    s[:, dc * 2048:(dc + 1) * 2048],
                    s[:, dc * 2048:(dc + 1) * 2048]),
                    deps=[t_mm.get(dc - 2), t_red.get(pi - 1),
                          t_sq.get(dc - 1)], inc="ev")

                def nmm(e, dc=dc, sl=sl):
                    for j in range(4):
                        inst = e.matmul(
                            PS0[0:16, j * 512:(j + 1) * 512],
                            selp[:, dc * 16:(dc + 1) * 16],
                            nsq[:, sl * 2048 + j * 512: sl * 2048 + (j + 1) * 512],
                            start=(dc == 0), stop=(dc == 7))
                    return inst

                t_mm[dc] = ph.op("tensor", nmm,
                                 deps=[t_sq[dc], t_red.get(pi - 1)], inc="mm")
            t_red[pi] = ph.op("vector", lambda e, pi=pi: e.reduce_max(
                n16[:, pi:pi + 1],
                PS0[0:16, :].rearrange("p (a t) -> p a t", a=4), axis=AX.XY),
                deps=[t_mm[7]], inc="ev")
        t_p = ph.op("vector", lambda e: e.tensor_mul(
            n16[:, 2:3], n16[:, 0:1], n16[:, 1:2]),
            deps=[t_red[1]], inc="ev", pre_drain=True)
        t_s = ph.op("scalar", lambda e: e.activation(
            n16[:, 3:4], n16[:, 2:3], AF.Sqrt), deps=[t_p], inc="sc")
        t_c = ph.op("vector", lambda e: e.tensor_scalar_mul(
            n16[:, 4:5], n16[:, 3:4], -1.01 * 0.125),
            deps=[t_s], inc="ev")
        t_t = ph.op("tensor", lambda e: e.matmul(
            PS1[0:1, 0:16], n16[0:16, 4:5], id16[:], start=True, stop=True),
            deps=[t_c], inc="mm")
        t_cr = ph.op("vector", lambda e: e.tensor_copy(
            crow[0:1, :], PS1[0:1, 0:16]), deps=[t_t], inc="ev")
        t_pm = ph.op("vector", lambda e: e.tensor_tensor(
            crow2[0:1, :],
            crow[0:1, :].rearrange("a (h two) -> a h two", two=2)[:, :, 0],
            crow[0:1, :].rearrange("a (h two) -> a h two", two=2)[:, :, 1],
            op=ALU.min), deps=[t_cr], inc="ev", pre_drain=True)
        t_bc = ph.op("tensor", lambda e: e.matmul(
            PS1[:, 512:520], ones_r[0:1, 0:128], crow2[0:1, :],
            start=True, stop=True), deps=[t_pm], inc="mm")
        return ph.op("vector", lambda e: e.tensor_copy(cb8[:], PS1[:, 512:520]),
                     deps=[t_bc], inc="ev")

    def attn_phase(seq, K_sb, Q_sb, V_sb, wq_s):
        nm = f"att{seq}"
        ph = Phase(P, nm)
        nsq = wq_s[:, 0:4096]
        x2o = wq_s[:, 4096:8192].bitcast(F32)
        with (
            nc.sbuf_tensor(nm + "_eS", [128, 2 * 16 * 1024], BF16) as expS,
            nc.sbuf_tensor(nm + "_x1q", [128, 4 * 1024], F32) as x1q,
            nc.sbuf_tensor(nm + "_x2b", [128, 4 * 1024], BF16) as x2b,
            nc.sbuf_tensor(nm + "_lb", [128, 16], F32) as lbuf,
        ):
            t_cb = emit_norms(ph, K_sb, Q_sb, nsq)
            t_x2add = {}
            t_store = {}
            t_exp = {}
            t_pvc = {}
            t_yev = {}
            t_x1 = {}
            pv_war = {0: t_cb, 1: t_cb}
            exp_last = {0: t_cb, 1: t_cb}
            pend = [None]

            def emit_pv_chunk(qg, hp, c):
                buf = (hp % 2) * 16384
                qi = c // 2
                half = c % 2
                qtp = qi // 2
                par = qi % 2
                h = hp * 2 + par

                def pv_fn(e, h=h, par=par, qtp=qtp, half=half, buf=buf):
                    for kt in range(half * 8, half * 8 + 8):
                        for q2 in range(2):
                            qt = qtp * 2 + q2
                            inst = e.matmul(
                                PS1[:, (par * 2 + q2) * 512:
                                     (par * 2 + q2) * 512 + 65],
                                expS[:, buf + kt * 1024 + par * 512 + qt * 128:
                                     buf + kt * 1024 + par * 512 + (qt + 1) * 128],
                                V_sb[:, kt * 1088 + h * 68: kt * 1088 + h * 68 + 65],
                                start=(kt == 0), stop=(kt == 15))
                    return inst

                deps = [t_exp[(qg, hp, 15)]]
                if half == 0 and par == 0:
                    if qtp == 1:
                        deps.append(t_yev.get((qg, hp, 0)))
                    else:
                        pq = (qg, hp - 1) if hp >= 1 else (qg - 1, 7)
                        deps.append(t_yev.get((pq[0], pq[1], 1)))
                        deps.append(t_cb)
                tok = ph.op("tensor", pv_fn, deps=deps, inc="mm")
                t_pvc[(qg, hp, c)] = tok
                pv_war[hp % 2] = tok
                if c in (3, 7):
                    qtp = (c - 3) // 4
                    pvdeps = [t_pvc[(qg, hp, qtp * 4 + k)] for k in range(4)]
                    t_l = ph.op("vector", lambda e, qtp=qtp: e.tensor_scalar(
                        out=lbuf[:, qtp * 4: qtp * 4 + 4][:, :, None],
                        in0=PS1[:, :].rearrange("p (a t) -> p a t", a=4)[:, :, 64:65],
                        scalar1=1e-30, scalar2=None, op0=ALU.max),
                        deps=pvdeps, inc="ev")
                    t_r = ph.op("vector", lambda e, qtp=qtp: e.reciprocal(
                        lbuf[:, 8 + qtp * 4: 8 + qtp * 4 + 4],
                        lbuf[:, qtp * 4: qtp * 4 + 4]),
                        deps=[t_l], inc="ev", pre_drain=True)
                    t_yev[(qg, hp, qtp)] = ph.op(
                        "vector", lambda e, hp=hp, qtp=qtp: e.tensor_mul(
                            x2b[:].rearrange("p (q2 hh t) -> p q2 hh t", q2=4, hh=16)[
                                :, 2 * qtp:2 * qtp + 2, 2 * hp:2 * hp + 2, :]
                            .transpose([0, 2, 1, 3]),
                            PS1[:, :].rearrange("p (par q2 t) -> p par q2 t",
                                                par=2, q2=2)[:, :, :, 0:64],
                            lbuf[:, 8 + qtp * 4: 8 + qtp * 4 + 4]
                            .rearrange("p (par q2) -> p par q2", par=2)[:, :, :, None]
                            .broadcast_to([128, 2, 2, 64])),
                        deps=[t_r, t_x2add.get((qg - 1, 3))], inc="ev")

            def drain_qg(qg):
                for qt in range(4):
                    fi = qg * 4 + qt
                    t_x2add[(qg, qt)] = ph.op("vector", lambda e, qt=qt: e.tensor_add(
                        x2o[:, (qt % 2) * 1024:(qt % 2 + 1) * 1024],
                        x2b[:, qt * 1024:(qt + 1) * 1024],
                        x1q[:, qt * 1024:(qt + 1) * 1024]),
                        deps=[t_yev[(qg, 7, qt // 2)], t_x1[qg],
                              t_store.get(fi - 2)], inc="ev")
                    t_store[fi] = ph.op("gpsimd", lambda e, fi=fi, qt=qt: e.dma_start(
                        x2s.ap()[fi * 128:(fi + 1) * 128, :],
                        x2o[:, (qt % 2) * 1024:(qt % 2 + 1) * 1024]),
                        deps=[t_x2add[(qg, qt)]], inc=ph.st(f"o{qt % 2}"), n=16)

            t_x1[0] = ph.op("sync", lambda e: e.dma_start(
                x1q[:].rearrange("p (a t) -> p a t", a=4),
                x1s.ap().rearrange("(a p) t -> p a t", p=128)[:, 0:4, :]),
                inc=ph.ld("xq"), n=16)
            for qg in range(4):
                for hp in range(8):
                    buf = (hp % 2) * 16384
                    for r in range(16):
                        pp = (r % 2) * 1024

                        def st_fn(e, hp=hp, r=r, qg=qg, pp=pp):
                            for par in range(2):
                                inst = e.matmul(
                                    PS0[:, pp + par * 512: pp + (par + 1) * 512],
                                    K_sb[64 * par:64 * par + 64,
                                         hp * 2048 + r * 128:
                                         hp * 2048 + (r + 1) * 128],
                                    Q_sb[64 * par:64 * par + 64,
                                         hp * 2048 + qg * 512:
                                         hp * 2048 + (qg + 1) * 512],
                                    start=True, stop=True)
                            return inst

                        t_mm = ph.op("tensor", st_fn, deps=[exp_last[r % 2]],
                                     inc="mm")
                        t_exp[(qg, hp, r)] = ph.op(
                            "scalar", lambda e, hp=hp, r=r, buf=buf, pp=pp: e.activation(
                                expS[:, buf + r * 1024: buf + (r + 1) * 1024],
                                PS0[:, pp:pp + 1024], AF.Exp,
                                bias=cb8[:, hp:hp + 1], scale=0.125),
                            deps=[t_mm, pv_war[hp % 2], t_cb], inc="sc")
                        exp_last[r % 2] = t_exp[(qg, hp, r)]
                        if pend[0] is not None and r % 2 == 1:
                            emit_pv_chunk(pend[0][0], pend[0][1], r // 2)
                    pend[0] = (qg, hp)
                    if hp == 0 and qg > 0:
                        drain_qg(qg - 1)
                        t_x1[qg] = ph.op("sync", lambda e, qg=qg: e.dma_start(
                            x1q[:].rearrange("p (a t) -> p a t", a=4),
                            x1s.ap().rearrange("(a p) t -> p a t", p=128)[
                                :, qg * 4: qg * 4 + 4, :]),
                            deps=[t_x2add[(qg - 1, 3)]], inc=ph.ld("xq"), n=16)
            for c in range(8):
                emit_pv_chunk(3, 7, c)
            drain_qg(3)
            flush(ph)
            ph.run()

    if "prep" in stages:
        prep()
        php = Phase(P, "prep_w")
        with (
            nc.sbuf_tensor("pp_ws", [128, 4 * HID], I32) as pws,
            nc.sbuf_tensor("pp_wb", [128, 4 * HID], BF16) as pwb,
            nc.sbuf_tensor("pp_wo", [128, 4 * HID], BF16) as pwo,
        ):
            conv = {}
            evac = {}
            stor = {}
            tps = {}
            ci = 0
            bank = [0]
            for (wsrc, wdstT, nch, cw) in ((wg_d, wgT, 24, D), (wu_d, wuT, 24, D),
                                           (wd_d, wdT, 8, HID)):
                ntile = cw // 128
                for ch in range(nch):
                    sl = ci % 4
                    t_l = php.op("sync", lambda e, s=wsrc, c=ch, sl=sl, cw=cw: e.dma_start(
                        pws[:, sl * HID: sl * HID + cw],
                        s.ap()[c * 128:(c + 1) * 128, :]),
                        deps=[conv.get(ci - 4)], inc=php.ld(f"w{sl}"), n=16)
                    conv[ci] = php.op("vector", lambda e, sl=sl, cw=cw: e.tensor_scalar_mul(
                        pwb[:, sl * HID: sl * HID + cw],
                        pws[:, sl * HID: sl * HID + cw], QS),
                        deps=[t_l, tps.get(ci - 4)], inc="ev")
                    nround = (ntile + 7) // 8
                    for rd in range(nround):
                        t0, t1 = rd * 8, min(ntile, rd * 8 + 8)
                        bk = bank[0] % 2
                        bank[0] += 1
                        tview = PS0[:, bk * 512:(bk + 1) * 512].bitcast(BF16)

                        def tp_fn(e, sl=sl, t0=t0, t1=t1, tview=tview):
                            for dc in range(t0, t1):
                                inst = e.transpose(
                                    tview[:, (dc - t0) * 128:(dc - t0 + 1) * 128],
                                    pwb[:, sl * HID + dc * 128: sl * HID + (dc + 1) * 128],
                                    ident[:])
                            return inst

                        t_tp = php.op("tensor", tp_fn,
                                      deps=[conv[ci], evac.get(bank[0] - 3)],
                                      inc="mm")
                        tps[ci] = t_tp
                        t_e = php.op("vector", lambda e, sl=sl, t0=t0, t1=t1, tview=tview: e.tensor_copy(
                            pwo[:, sl * HID + t0 * 128: sl * HID + t1 * 128],
                            tview[:, 0:(t1 - t0) * 128]),
                            deps=[t_tp, stor.get(ci - 4)], inc="ev")
                        evac[(ci, rd)] = t_e
                        evac[bank[0] - 1] = t_e
                    if cw == D:
                        def st_ap(c):
                            return bass.AP(
                                wdstT, (c // 2) * 262144 + (c % 2) * 128,
                                [[256, 128], [32768, 8], [1, 128]])
                        out_ap = st_ap(ch)
                    else:
                        out_ap = bass.AP(
                            wdT, ch * 128, [[1024, 128], [131072, 24], [1, 128]])
                    edeps = [evac[(ci, rd)] for rd in range(nround)]
                    stor[ci] = php.op("gpsimd", lambda e, sl=sl, cw=cw, out_ap=out_ap: e.dma_start(
                        out_ap,
                        pwo[:, sl * HID: sl * HID + cw].rearrange(
                            "p (a h) -> p a h", a=cw // 128)),
                        deps=edeps, inc=php.st(f"pb{sl}"), n=16)
                    ci += 1
            flush(php)
            php.run()
    for s in range(NSEQ):
        es2 = ExitStack()
        wq_s = bbq = None
        if {"a2", "b"} & set(stages):
            wq_s = es2.enter_context(nc.sbuf_tensor(f"Wqh{s}", [128, 8 * 1024], BF16))
            bbq = es2.enter_context(nc.sbuf_tensor(f"bbq{s}", [1, 1024], BF16))
        if "a1" in stages:
            ffn_phase(f"a1s{s}", x_d.ap()[s], x1s.ap(),
                      wprep2=((wq_s, bbq) if wq_s is not None else None))
            if debug and s == 0:
                phd = Phase(P, "dbgsm")
                for i, buf in enumerate((sm_s, sm_q, sm_m, sm_r)):
                    phd.op("gpsimd", lambda e, i=i, buf=buf: e.dma_start(
                        dbg_sm.ap()[:, i * 16:(i + 1) * 16], buf[:]),
                        inc=phd.st(f"d{i}"), n=16)
                flush(phd)
                phd.run()
        if {"a2", "norm", "b"} & set(stages):
            with (
                nc.sbuf_tensor(f"K{s}", [128, 8 * 2048], BF16) as K_sb,
                nc.sbuf_tensor(f"Q{s}", [128, 8 * 2048], BF16) as Q_sb,
                nc.sbuf_tensor(f"V{s}", [128, 16 * 1088], BF16) as V_sb,
            ):
                if "a2" in stages:
                    qkv_phase(s, K_sb, Q_sb, V_sb, wq_s, bbq)
                if "b" in stages:
                    attn_phase(s, K_sb, Q_sb, V_sb, wq_s)
        es2.close()
        if "c" in stages:
            ffn_phase(f"cs{s}", x2s.ap(), out_d.ap()[s])
        if "one_seq" in stages:
            break

    ph = Phase(P, "fin")
    for i in range(8):
        ph.op("gpsimd", lambda e, l=f"st{i}": e.wait_ge(sems[l], P.C[l]))
    ph.run()

    es.close()
    return nc


_NC_CACHE = [None]


def kernel(**inputs):
    x = np.ascontiguousarray(np.asarray(inputs["L0_attn_x"], dtype=np.float32))
    full = {k: np.ascontiguousarray(np.asarray(v)) for k, v in inputs.items()
            if k != "L0_attn_x"}
    if _NC_CACHE[0] is None:
        _NC_CACHE[0] = build_program()
    nc = _NC_CACHE[0]
    in_maps = []
    for c in range(8):
        m = dict(full)
        m["L0_attn_x"] = x[c * NSEQ:(c + 1) * NSEQ]
        in_maps.append(m)
    res = run_bass_kernel_spmd(nc, in_maps, core_ids=list(range(8)))
    out = np.concatenate([np.asarray(r["out"]) for r in res.results], axis=0)
    return out.astype(np.float32)


# version 25
# speedup vs baseline: 1.0678x; 1.0542x over previous
import numpy as np
from contextlib import ExitStack

import concourse.bass as bass
import concourse.mybir as mybir
from concourse.bass_utils import run_bass_kernel_spmd

F32 = mybir.dt.float32
BF16 = mybir.dt.bfloat16
I32 = mybir.dt.int32
AF = mybir.ActivationFunctionType
ALU = mybir.AluOpType
AX = mybir.AxisListType

NSEQ = 2
T = 2048
D = 1024
H = 16
HID = 3072
NT = T // 128
EPS = 1e-5
QS = 1.0 / 127.0

ENGINES = ("sync", "gpsimd", "vector", "scalar", "tensor")


class Prog:
    def __init__(self, nc, sems):
        self.nc = nc
        self.sems = sems
        self.C = {k: 0 for k in sems}
        self.epoch = 0
        self.waited = {}


class Phase:
    def __init__(self, prog, name):
        self.p = prog
        self.name = name
        self.q = {e: [] for e in ENGINES}
        self._ld = {}
        self._st = {}
        self._sd = {}

    def ld(self, chain):
        if chain not in self._ld:
            assert len(self._ld) < 16, "out of ld lanes"
            self._ld[chain] = f"ld{len(self._ld)}"
        return self._ld[chain]

    def st(self, chain):
        if chain not in self._st:
            assert len(self._st) < 8, "out of st lanes"
            self._st[chain] = f"st{len(self._st)}"
        return self._st[chain]

    def sd(self, chain):
        if chain not in self._sd:
            assert len(self._sd) < 4, "out of sl lanes"
            self._sd[chain] = f"sl{len(self._sd)}"
        return self._sd[chain]

    def op(self, eng, fn, deps=(), inc=None, n=1, pre_nops=0, pre_drain=False):
        p = self.p
        waits = {}
        for t in deps:
            if t is None:
                continue
            sem, val, ep, src = t
            if ep < p.epoch:
                continue
            if val > waits.get(sem, -1):
                waits[sem] = val
        wl = []
        for sem, val in waits.items():
            key = (eng, sem)
            if p.waited.get(key, -1) >= val:
                continue
            p.waited[key] = val
            wl.append((sem, val))
        tok = None
        if inc is not None:
            p.C[inc] += n
            tok = (inc, p.C[inc], p.epoch, eng)
        self.q[eng].append((fn, wl, inc, n, pre_nops, pre_drain))
        return tok

    def run(self):
        p = self.p
        nc = p.nc
        with nc.Block(self.name) as blk:
            for eng_name in ENGINES:
                ops = self.q[eng_name]
                if not ops:
                    continue

                def body(e, ops=ops, eng_name=eng_name):
                    for fn, wl, inc, n, pre_nops, pre_drain in ops:
                        for sem, val in wl:
                            e.wait_ge(p.sems[sem], val)
                        if pre_drain:
                            e.drain()
                        inst = fn(e)
                        if inc is not None:
                            inst.then_inc(p.sems[inc], n)

                getattr(blk, eng_name)(body)
        p.epoch += 1


def build_program(debug=False, stages=("prep", "a1", "a2", "norm", "b", "c")):
    nc = bass.Bass("TRN2", target_bir_lowering=False)

    x_d = nc.dram_tensor("L0_attn_x", [NSEQ, T, D], F32, kind="ExternalInput")
    g_d = nc.dram_tensor("L0_attn_ln_g", [D], F32, kind="ExternalInput")
    b_d = nc.dram_tensor("L0_attn_ln_b", [D], F32, kind="ExternalInput")
    Wq_d = nc.dram_tensor("L0_attn_Wq", [D, D], F32, kind="ExternalInput")
    bq_d = nc.dram_tensor("L0_attn_bq", [D], F32, kind="ExternalInput")
    Wk_d = nc.dram_tensor("L0_attn_Wk", [D, D], F32, kind="ExternalInput")
    bk_d = nc.dram_tensor("L0_attn_bk", [D], F32, kind="ExternalInput")
    Wv_d = nc.dram_tensor("L0_attn_Wv", [D, D], F32, kind="ExternalInput")
    bv_d = nc.dram_tensor("L0_attn_bv", [D], F32, kind="ExternalInput")
    wg_d = nc.dram_tensor("L0_ffn_w_gate", [HID, D], I32, kind="ExternalInput")
    wu_d = nc.dram_tensor("L0_ffn_w_up", [HID, D], I32, kind="ExternalInput")
    wd_d = nc.dram_tensor("L0_ffn_w_down", [D, HID], I32, kind="ExternalInput")
    out_d = nc.dram_tensor("out", [NSEQ, T, D], F32, kind="ExternalOutput")

    sk = "ExternalOutput" if debug else "Internal"
    wg16 = nc.dram_tensor("wg16", [HID, D], BF16, kind="Internal")
    wu16 = nc.dram_tensor("wu16", [HID, D], BF16, kind="Internal")
    wd16 = nc.dram_tensor("wd16", [D, HID], BF16, kind="Internal")
    wgT = nc.dram_tensor("wgT", [12 * 8 * 128 * 256], BF16, kind="Internal")
    wuT = nc.dram_tensor("wuT", [12 * 8 * 128 * 256], BF16, kind="Internal")
    wdT = nc.dram_tensor("wdT", [HID, D], BF16, kind="Internal")
    x1s = nc.dram_tensor("x1s", [T, D], F32, kind=sk)
    dbg_sm = nc.dram_tensor("dbg_sm", [128, 64], F32, kind=sk)
    x2s = nc.dram_tensor("x2s", [T, D], F32, kind=sk)

    es = ExitStack()
    sems = {}
    for s in (["mm", "ev", "sc", "gp"] + [f"ld{i}" for i in range(16)]
              + [f"st{i}" for i in range(8)] + [f"sl{i}" for i in range(4)]):
        sems[s] = es.enter_context(nc.semaphore(s))
    P = Prog(nc, sems)

    SB = lambda name, shape, dt: es.enter_context(nc.sbuf_tensor(name, shape, dt))
    ident = SB("ident", [128, 128], BF16)
    id16 = SB("id16", [16, 16], F32)
    ones_r = SB("ones_r", [1, 512], BF16)
    selp = SB("selp", [128, 128], BF16)
    gch = SB("gch", [128, 8], F32)
    bch = SB("bch", [128, 8], F32)
    bchb = SB("bchb", [128, 8], BF16)
    cb = SB("cb", [128, 16], F32)
    cb8 = SB("cb8", [128, 8], F32)
    crow = SB("crow", [1, 16], BF16)
    crow2 = SB("crow2", [1, 8], BF16)
    n16 = SB("n16", [16, 8], F32)
    sm_s = SB("sm_s", [128, 16], F32)
    sm_q = SB("sm_q", [128, 16], F32)
    sm_m = SB("sm_m", [128, 16], F32)
    sm_r = SB("sm_r", [128, 16], F32)
    sm2_s = SB("sm2_s", [128, 16], F32)
    sm2_q = SB("sm2_q", [128, 16], F32)
    sm2_m = SB("sm2_m", [128, 16], F32)
    sm2_r = SB("sm2_r", [128, 16], F32)
    sm_t1 = SB("sm_t1", [128, 16], F32)
    sm_t2 = SB("sm_t2", [128, 16], F32)
    eps_c = SB("eps_c", [128, 1], F32)
    ones_f = SB("ones_f", [1, 1], F32)

    PS0 = es.enter_context(nc.psum_tensor("PS0", [128, 2048], F32))
    PS1 = es.enter_context(nc.psum_tensor("PS1", [128, 2048], F32))

    def flush(ph):
        for lane in list(ph._ld.values()):
            ph.op("sync", lambda e, l=lane: e.wait_ge(sems[l], P.C[l]))
        for lane in list(ph._st.values()):
            ph.op("gpsimd", lambda e, l=lane: e.wait_ge(sems[l], P.C[l]))
        for lane in list(ph._sd.values()):
            ph.op("scalar", lambda e, l=lane: e.wait_ge(sems[l], P.C[l]))

    def stats_chain(ph, b, t_red, t_sq, prev_done, bufs=None):
        ss, qq, mm_, rr = bufs if bufs else (sm_s, sm_q, sm_m, sm_r)
        c0, c1 = b * 4, b * 4 + 4
        deps = ([t_red[b * 4 + i] for i in range(4)] if t_red
                else [prev_done])
        t_m = ph.op("scalar", lambda e: e.mul(mm_[:, c0:c1], ss[:, c0:c1], 1.0 / D),
                    deps=deps, inc="sc")
        t_m2 = ph.op("scalar", lambda e: e.activation(
            sm_t1[:, c0:c1], mm_[:, c0:c1], AF.Square),
            deps=[prev_done, t_m], inc="sc")
        t_var = ph.op("vector", lambda e: e.scalar_tensor_tensor(
            out=sm_t2[:, c0:c1], in0=qq[:, c0:c1], scalar=1.0 / D,
            in1=sm_t1[:, c0:c1], op0=ALU.mult, op1=ALU.subtract),
            deps=[t_m2] + ([t_sq[b * 4 + i] for i in range(4)] if t_sq else []),
            inc="ev")
        t_sd = ph.op("scalar", lambda e: e.activation(
            sm_t1[:, c0:c1], sm_t2[:, c0:c1], AF.Sqrt, bias=eps_c[:, 0:1]),
            deps=[t_var], inc="sc")
        return ph.op("vector", lambda e: e.reciprocal(
            rr[:, c0:c1], sm_t1[:, c0:c1]), deps=[t_sd], inc="ev")

    def prep():
        ph = Phase(P, "prep_const")

        def mset(buf, val):
            return ph.op("gpsimd", lambda e, buf=buf, val=val: e.memset(buf, val),
                         inc="gp")

        def asel(buf, dep, cmp, fill, base, pat, cm):
            return ph.op("gpsimd", lambda e, buf=buf: e.affine_select(
                out=buf, in_=buf, compare_op=cmp, fill=fill, base=base,
                pattern=pat, channel_multiplier=cm), deps=[dep], inc="gp")

        t0 = mset(ident[:], 0.0)
        asel(ident[:], t0, ALU.not_equal, 1.0, 0, [[-1, 128]], 1)
        t0 = mset(id16[:], 0.0)
        asel(id16[:], t0, ALU.not_equal, 1.0, 0, [[-1, 16]], 1)
        mset(ones_r[:], 1.0)
        mset(eps_c[:], EPS)
        t_of = mset(ones_f[:], 1.0)
        for dc in range(8):
            sl_ = selp[:, dc * 16:(dc + 1) * 16]
            t0 = mset(sl_, 1.0)
            t1 = asel(sl_, t0, ALU.is_ge, 0.0, 128 * dc, [[-64, 16]], 1)
            asel(sl_, t1, ALU.is_ge, 0.0, 63 - 128 * dc, [[64, 16]], -1)
        ph.run()

        ph = Phase(P, "prep_gb")
        with nc.sbuf_tensor("pgb_row", [1, 2048], F32) as grow:
            t_g = ph.op("sync", lambda e: e.dma_start(
                grow[0:1, 0:1024], g_d.ap().rearrange("(a d) -> a d", a=1)),
                inc=ph.ld("ga"), n=16)
            t_b = ph.op("sync", lambda e: e.dma_start(
                grow[0:1, 1024:2048], b_d.ap().rearrange("(a d) -> a d", a=1)),
                inc=ph.ld("gb"), n=16)

            def gmm(e):
                for dc in range(16):
                    inst = e.matmul(PS0[:, dc:dc + 1],
                                    grow[0:1, dc * 128:(dc + 1) * 128],
                                    ones_f[0:1, 0:1], start=True, stop=True)
                return inst

            t_mm = ph.op("tensor", gmm, deps=[t_g, t_b, t_of], inc="mm")
            ph.op("vector", lambda e: e.tensor_copy(gch[:], PS0[:, 0:8]),
                  deps=[t_mm], inc="ev")
            t_bc2 = ph.op("vector", lambda e: e.tensor_copy(bch[:], PS0[:, 8:16]),
                          deps=[t_mm], inc="ev")
            ph.op("scalar", lambda e: e.copy(bchb[:], bch[:]),
                  deps=[t_bc2], inc="sc")
            ph.run()

    def emit_prep(ph, wstg, wbf, tpb, nslot=1):
        conv = {}
        stor = {}
        lastw = {}
        ci = 0
        for (wsrc, wdst, nch, cw) in ((wg_d, wg16, 24, D), (wu_d, wu16, 24, D),
                                      (wd_d, wd16, 8, HID)):
            for ch in range(nch):
                bdep2 = None
                if cw == D:
                    off = (ci % 2) * 1024
                    bdep = stor.get(ci - 2)
                else:
                    off = 0
                    bdep = stor.get(ci - 1)
                    bdep2 = stor.get(ci - 2)
                soff = (ci % nslot) * HID
                t_l = ph.op("gpsimd", lambda e, s=wsrc, c=ch, cw=cw, soff=soff: e.dma_start(
                    wstg[:, soff:soff + cw],
                    s.ap()[c * 128:(c + 1) * 128, :]),
                    deps=[conv.get(ci - nslot)], inc=ph.st(f"pw{ci % nslot}"), n=16)
                conv[ci] = ph.op("vector", lambda e, cw=cw, off=off, soff=soff: e.tensor_scalar_mul(
                    wbf[:, off: off + cw], wstg[:, soff:soff + cw], QS),
                    deps=[t_l, bdep, bdep2], inc="ev")
                stor[ci] = ph.op("gpsimd", lambda e, d=wdst, c=ch, cw=cw, off=off: e.dma_start(
                    d.ap()[c * 128:(c + 1) * 128, :], wbf[:, off: off + cw]),
                    deps=[conv[ci]], inc=ph.st(f"pb{(ci % 2) if cw == D else 0}"), n=16)
                ci += 1
            lastw[len(lastw)] = [stor[ci - 1], stor.get(ci - 2)]
        tlast = {}
        ready = []
        ti_ = 0
        for wi, (wsrc16, wdstT) in enumerate(((wg16, wgT), (wu16, wuT))):
            for dc in range(8):
                sl = ti_ % 2
                t_t = ph.op("scalar", lambda e, s=wsrc16, dc=dc, sl=sl: e.dma_start(
                    tpb[:, sl * HID:(sl + 1) * HID],
                    s.ap()[:, dc * 128:(dc + 1) * 128], transpose=True),
                    deps=[tlast.get(sl)] + lastw[wi], inc=ph.sd(f"pt{sl}"), n=16)
                tlast[sl] = ph.op("gpsimd", lambda e, d=wdstT, dc=dc, sl=sl: e.dma_start(
                    d.ap().rearrange("(a b p h) -> b p a h", a=12, b=8, p=128)[dc],
                    tpb[:, sl * HID:(sl + 1) * HID].rearrange("p (a h) -> p a h", a=12)),
                    deps=[t_t], inc=ph.st(f"pp{sl}"), n=16)
                ti_ += 1
            ready.append([tlast.get(0), tlast.get(1)])
        for hc in range(24):
            sl = ti_ % 2
            t_t = ph.op("scalar", lambda e, hc=hc, sl=sl: e.dma_start(
                tpb[:, sl * HID: sl * HID + D],
                wd16.ap()[:, hc * 128:(hc + 1) * 128], transpose=True),
                deps=[tlast.get(sl)] + lastw[2], inc=ph.sd(f"pt{sl}"), n=16)
            tlast[sl] = ph.op("gpsimd", lambda e, hc=hc, sl=sl: e.dma_start(
                wdT.ap()[hc * 128:(hc + 1) * 128, :],
                tpb[:, sl * HID: sl * HID + D]),
                deps=[t_t], inc=ph.st(f"pp{sl}"), n=16)
            ti_ += 1
        ready.append([tlast.get(0), tlast.get(1)])
        return ready

    def ffn_phase(name, src_ap, dst_ap, with_prep=False, next_stats=False):
        ph = Phase(P, name)
        wready = [[], [], []]
        pes = ExitStack()
        if with_prep:
            wstg = pes.enter_context(nc.sbuf_tensor(name + "_pws", [128, HID], I32))
            wbf = pes.enter_context(nc.sbuf_tensor(name + "_pwb", [128, HID + 1024], BF16))
            tpb = pes.enter_context(nc.sbuf_tensor(name + "_pwt", [128, 2 * HID], BF16))
        nslots = 8
        with (
            nc.sbuf_tensor(name + "_xst", [128, nslots * 1024], F32) as xst,
            nc.sbuf_tensor(name + "_xs2", [128, 4 * 1024], F32) as xst2,
            nc.sbuf_tensor(name + "_xntm", [128, 2 * 1024], BF16) as xntm,
            nc.sbuf_tensor(name + "_xnfm", [128, 8 * 1024], BF16) as xnfm,
            nc.sbuf_tensor(name + "_hsb", [128, 24 * 1024], BF16) as hsb,
            nc.sbuf_tensor(name + "_wslg", [128, 2 * 2048], BF16) as wslg,
            nc.sbuf_tensor(name + "_wslu", [128, 2 * 2048], BF16) as wslu,
            nc.sbuf_tensor(name + "_wdsb", [128, 24 * 1024], BF16) as wdsb,
            nc.sbuf_tensor(name + "_sq", [128, 1024], BF16) as sqscr,
            nc.sbuf_tensor(name + "_sg", [128, 2 * 512], BF16) as sgb,
            nc.sbuf_tensor(name + "_xo", [128, (1 if with_prep else 2) * 1024], F32) as x1o,
        ):
            if with_prep:
                wready = emit_prep(ph, wstg, wbf, tpb)
            t_wd_holder = [None]

            t_load = [None] * NT
            t_xn = [None] * NT
            t_ev = [None] * NT
            t_sq = [None] * NT
            t_red = [None] * NT
            t_tp = [None] * NT
            t_statsdone = [None] * 4
            t_evac_bank = [None, None]
            t_gu_mm = {}
            t_silu = {}
            t_hmul = {}
            t_down = {}
            t_stt = {}
            t_red2 = [None] * NT
            t_sq2 = [None] * NT
            t_store = [None] * NT
            t_slab = {}
            gu_last = [None, None]

            for g in range(2):
                for ti in range(8):
                    tt = g * 8 + ti
                    slot = tt % nslots
                    if tt < 12:
                        xs_, xo_ = xst, slot * 1024
                        t_load[tt] = ph.op("sync", lambda e, tt=tt, slot=slot: e.dma_start(
                            xst[:, slot * 1024:(slot + 1) * 1024],
                            src_ap[tt * 128:(tt + 1) * 128, :]),
                            deps=([t_xn[tt - 8], t_sq[tt - 8],
                                   t_stt.get((0, tt - 8, 1))]
                                  if tt >= 8 else []),
                            inc=ph.ld(f"x{slot}"), n=16)
                    else:
                        xs_, xo_ = xst2, (tt - 12) * 1024
                        t_load[tt] = ph.op("sync", lambda e, tt=tt: e.dma_start(
                            xst2[:, (tt - 12) * 1024:(tt - 11) * 1024],
                            src_ap[tt * 128:(tt + 1) * 128, :]),
                            inc=ph.ld(f"y{tt - 12}"), n=16)
                    if tt < 8:
                        t_red[tt] = ph.op("vector", lambda e, tt=tt, xs_=xs_, xo_=xo_: e.reduce_sum(
                            sm_s[:, tt:tt + 1], xs_[:, xo_:xo_ + 1024],
                            axis=AX.X), deps=[t_load[tt]], inc="ev")
                    else:
                        t_red[tt] = ph.op("scalar", lambda e, tt=tt, xs_=xs_, xo_=xo_: e.activation(
                            sqscr[:], xs_[:, xo_:xo_ + 1024], AF.Identity,
                            accum_out=sm_s[:, tt:tt + 1]),
                            deps=[t_load[tt],
                                  (t_sq2[7] if next_stats else t_sq[7])
                                  if tt == 8 else t_sq[tt - 1]],
                            inc="sc")
                    t_sq[tt] = ph.op("scalar", lambda e, tt=tt, xs_=xs_, xo_=xo_: e.activation(
                        sqscr[:], xs_[:, xo_:xo_ + 1024], AF.Square,
                        accum_out=sm_q[:, tt:tt + 1]),
                        deps=[t_load[tt],
                              t_red[tt] if tt >= 8
                              else (t_sq[tt - 1] if tt >= 1 else None)],
                        inc="sc")
                    if tt % 4 == 3:
                        b_ = tt // 4
                        t_statsdone[b_] = stats_chain(
                            ph, b_, t_red, t_sq,
                            t_statsdone[b_ - 1] if b_ >= 1 else None)

                for ti in range(8):
                    tt = g * 8 + ti
                    slot = tt % nslots
                    xs_ = xst2 if tt >= 12 else xst
                    xo_ = (tt - 12 if tt >= 12 else slot) * 1024
                    t_xn[tt] = ph.op("vector", lambda e, tt=tt, xs_=xs_, xo_=xo_: e.tensor_scalar(
                        out=xntm[:, (tt % 2) * 1024:(tt % 2 + 1) * 1024],
                        in0=xs_[:, xo_:xo_ + 1024],
                        scalar1=sm_m[:, tt:tt + 1], scalar2=sm_r[:, tt:tt + 1],
                        op0=ALU.subtract, op1=ALU.mult),
                        deps=[t_statsdone[tt // 4],
                              t_tp[tt - 2] if tt >= 2 else None],
                        inc="ev")
                    bank = tt % 2
                    tview = PS0[:, bank * 512:(bank + 1) * 512].bitcast(BF16)

                    def tp_fn(e, tt=tt, tview=tview):
                        for dc in range(8):
                            inst = e.transpose(
                                tview[:, dc * 128:(dc + 1) * 128],
                                xntm[:, (tt % 2) * 1024 + dc * 128:
                                     (tt % 2) * 1024 + (dc + 1) * 128],
                                ident[:])
                        return inst

                    t_tp[tt] = ph.op("tensor", tp_fn,
                                     deps=[t_xn[tt], t_evac_bank[bank]], inc="mm")
                    t_ev[tt] = ph.op("vector", lambda e, tview=tview, ti=ti: e.tensor_copy(
                        xnfm[:].rearrange("p (a t) -> p a t", a=8)[:, :, ti * 128:(ti + 1) * 128],
                        tview[:].rearrange("p (a t) -> p a t", a=8)),
                        deps=[t_tp[tt], gu_last[g - 1] if g >= 1 else None],
                        inc="ev")
                    t_evac_bank[bank] = t_ev[tt]

                grp_ev_h = {0: t_ev[g * 8 + 3], 1: t_ev[g * 8 + 7]}

                for hc2 in range(12):
                    sl = hc2 % 2
                    if g == 0 and hc2 < 2:
                        slab_dep = None
                    elif hc2 >= 2:
                        slab_dep = t_gu_mm.get((g, hc2 - 2, 1, 1))
                    else:
                        slab_dep = t_gu_mm.get((g - 1, 10 + hc2, 1, 1))
                    t_lg = ph.op("sync", lambda e, hc2=hc2, sl=sl: e.dma_start(
                        wslg[:, sl * 2048:(sl + 1) * 2048].rearrange("p (b h) -> p b h", b=8),
                        wgT.ap().rearrange("(a b p h) -> a p b h", a=12, b=8, p=128)[hc2]),
                        deps=[slab_dep] + wready[0], inc=ph.ld(f"gu{sl}"), n=16)
                    t_lu = ph.op("sync", lambda e, hc2=hc2, sl=sl: e.dma_start(
                        wslu[:, sl * 2048:(sl + 1) * 2048].rearrange("p (b h) -> p b h", b=8),
                        wuT.ap().rearrange("(a b p h) -> a p b h", a=12, b=8, p=128)[hc2]),
                        deps=[slab_dep] + wready[1], inc=ph.ld(f"gu{sl}"), n=16)
                    for hcl in range(2):
                        hc = hc2 * 2 + hcl
                        for th in range(2):
                            k = hc * 2 + th
                            par = k % 2
                            gview = PS0[:, 1024:1536] if par == 0 else PS1[:, 0:512]
                            uview = PS0[:, 1536:2048] if par == 0 else PS1[:, 512:1024]

                            def gu_fn(e, hcl=hcl, th=th, sl=sl, gview=gview, uview=uview):
                                for dc in range(8):
                                    lg = wslg[:, sl * 2048 + dc * 256 + hcl * 128:
                                              sl * 2048 + dc * 256 + (hcl + 1) * 128]
                                    rhs = xnfm[:, dc * 1024 + th * 512:
                                               dc * 1024 + (th + 1) * 512]
                                    e.matmul(gview, lg, rhs, start=(dc == 0), stop=(dc == 7))
                                for dc in range(8):
                                    lu = wslu[:, sl * 2048 + dc * 256 + hcl * 128:
                                              sl * 2048 + dc * 256 + (hcl + 1) * 128]
                                    rhs = xnfm[:, dc * 1024 + th * 512:
                                               dc * 1024 + (th + 1) * 512]
                                    inst = e.matmul(uview, lu, rhs, start=(dc == 0), stop=(dc == 7))
                                return inst

                            t_mmgu = ph.op("tensor", gu_fn,
                                           deps=[t_lg, t_lu, grp_ev_h[th],
                                                 t_silu.get((g, k - 2)),
                                                 t_hmul.get((g, k - 2))],
                                           inc="mm")
                            t_gu_mm[(g, hc2, hcl, th)] = t_mmgu
                            gu_last[g] = t_mmgu
                            t_silu[(g, k)] = ph.op(
                                "scalar", lambda e, gview=gview, th=th: e.activation(
                                    sgb[:, (th % 2) * 512:(th % 2 + 1) * 512],
                                    gview, AF.Silu),
                                deps=[t_mmgu, t_hmul.get((g, k - 2)),
                                      t_silu.get((g, k - 2))], inc="sc")
                            t_hmul[(g, k)] = ph.op(
                                "vector", lambda e, uview=uview, hc=hc, th=th: e.tensor_mul(
                                    hsb[:, hc * 1024 + th * 512: hc * 1024 + (th + 1) * 512],
                                    uview, sgb[:, (th % 2) * 512:(th % 2 + 1) * 512]),
                                deps=[t_silu[(g, k)], t_mmgu,
                                      t_down.get((g - 1, 7, 1))], inc="ev")

                hmul_done = t_hmul[(g, 47)]
                if g == 0:
                    t_wd_holder[0] = ph.op("sync", lambda e: e.dma_start(
                        wdsb[:].rearrange("p (a h) -> p a h", a=24),
                        wdT.ap().rearrange("(a p) h -> p a h", p=128)),
                        deps=wready[2], inc=ph.ld("wd"), n=16)
                t_wd = t_wd_holder[0]

                for ti in range(8):
                    tt = g * 8 + ti
                    for oh in range(2):
                        dview = PS1[:, 1024 + oh * 512: 1536 + oh * 512]

                        def dn_fn(e, ti=ti, oh=oh, dview=dview):
                            for hc in range(24):
                                inst = e.matmul(
                                    dview,
                                    hsb[:, hc * 1024 + ti * 128: hc * 1024 + (ti + 1) * 128],
                                    wdsb[:, hc * 1024 + oh * 512: hc * 1024 + (oh + 1) * 512],
                                    start=(hc == 0), stop=(hc == 23))
                            return inst

                        t_down[(g, ti, oh)] = ph.op(
                            "tensor", dn_fn,
                            deps=[hmul_done, t_wd if (g == 0) else None,
                                  t_stt.get((g, ti - 1, oh)),
                                  t_stt.get((g - 1, 7, oh))],
                            inc="mm")
                        xom = 1 if with_prep else 2
                        rs_ = xst2 if tt >= 12 else xst
                        ro_ = (tt - 12 if tt >= 12 else tt % nslots) * 1024
                        t_stt[(g, ti, oh)] = ph.op(
                            "vector", lambda e, ti=ti, oh=oh, dview=dview, xom=xom,
                            rs_=rs_, ro_=ro_: e.scalar_tensor_tensor(
                                out=x1o[:, (ti % xom) * 1024 + oh * 512:
                                        (ti % xom) * 1024 + (oh + 1) * 512],
                                in0=dview, scalar=0.5,
                                in1=rs_[:, ro_ + oh * 512:ro_ + (oh + 1) * 512],
                                op0=ALU.mult, op1=ALU.add),
                            deps=[t_down[(g, ti, oh)],
                                  t_store[tt - xom] if tt >= xom else None,
                                  t_red2[tt - xom] if tt >= xom else None,
                                  t_sq2[tt - xom] if tt >= xom else None],
                            inc="ev")
                    t_store[tt] = ph.op("gpsimd", lambda e, tt=tt, ti=ti, xom=(1 if with_prep else 2): e.dma_start(
                        dst_ap[tt * 128:(tt + 1) * 128, :],
                        x1o[:, (ti % xom) * 1024:(ti % xom + 1) * 1024]),
                        deps=[t_stt[(g, ti, 0)], t_stt[(g, ti, 1)]],
                        inc=ph.st(f"o{ti % 2}"), n=16)
                    if next_stats:
                        xsl_ = x1o[:, (ti % 2) * 1024:(ti % 2 + 1) * 1024]
                        t_red2[tt] = ph.op(
                            "vector", lambda e, tt=tt, xsl_=xsl_: e.reduce_sum(
                                sm2_s[:, tt:tt + 1], xsl_, axis=AX.X),
                            deps=[t_stt[(g, ti, 1)]], inc="ev")
                        t_sq2[tt] = ph.op(
                            "scalar", lambda e, tt=tt, xsl_=xsl_: e.activation(
                                sqscr[:], xsl_, AF.Square,
                                accum_out=sm2_q[:, tt:tt + 1]),
                            deps=[t_stt[(g, ti, 1)],
                                  t_sq2[tt - 1] if tt >= 1 else None,
                                  t_sq[15 if g == 1 else 7]],
                            inc="sc")

            flush(ph)
            ph.run()
        pes.close()

    def qkv_phase(seq, K_sb, Q_sb, V_sb, ext_stats=False):
        nm = f"a2s{seq}"
        ph = Phase(P, nm)
        with (
            nc.sbuf_tensor(nm + "_Wq", [128, 8 * 1024], BF16) as Wqs,
            nc.sbuf_tensor(nm + "_Wk", [128, 8 * 1024], BF16) as Wks,
            nc.sbuf_tensor(nm + "_Wv", [128, 8 * 1024], BF16) as Wvs,
            nc.sbuf_tensor(nm + "_bvec", [1, 1024], F32) as bvec,
            nc.sbuf_tensor(nm + "_bb16", [1, 3 * 1024], BF16) as bb16,
            nc.sbuf_tensor(nm + "_xst", [128, 6 * 1024], F32) as xst2,
            nc.sbuf_tensor(nm + "_xntm", [128, 2 * 1024], BF16) as xntm2,
            nc.sbuf_tensor(nm + "_xnfm", [128, 2 * 8 * 512], BF16) as xnfm2,
            nc.sbuf_tensor(nm + "_sq", [128, 1024], BF16) as sqscr2,
        ):
            cast = {}
            ci = 0
            for wi, (wsrc, wdst) in enumerate(((Wq_d, Wqs), (Wk_d, Wks), (Wv_d, Wvs))):
                for dc in range(8):
                    cast[ci] = ph.op("gpsimd", lambda e, s=wsrc, d=wdst, dc=dc: e.dma_start(
                        d[:, dc * 1024:(dc + 1) * 1024],
                        s.ap()[dc * 128:(dc + 1) * 128, :]),
                        inc=ph.st(f"c{wi}"), n=16)
                    ci += 1
            t_ba_prev = None
            t_bc_prev = None
            t_fold = {}
            for wi, (wsb, bsrc) in enumerate(((Wqs, bq_d), (Wks, bk_d), (Wvs, bv_d))):
                t_lb = ph.op("sync", lambda e, s=bsrc: e.dma_start(
                    bvec[0:1, :], s.ap().rearrange("(a d) -> a d", a=1)),
                    deps=[t_ba_prev], inc=ph.ld("bv"), n=16)
                def bmm(e, wsb=wsb):
                    for oi in range(2):
                        for dc in range(8):
                            inst = e.matmul(
                                PS0[0:1, oi * 512:(oi + 1) * 512],
                                bchb[:, dc:dc + 1],
                                wsb[:, dc * 1024 + oi * 512: dc * 1024 + (oi + 1) * 512],
                                start=(dc == 0), stop=(dc == 7))
                    return inst

                t_bm = ph.op("tensor", bmm,
                             deps=[cast[wi * 8 + dc] for dc in range(8)] + [t_ba_prev],
                             inc="mm")
                t_ba = ph.op("vector", lambda e, wi=wi: e.tensor_add(
                    bb16[0:1, wi * 1024:(wi + 1) * 1024],
                    PS0[0:1, 0:1024], bvec[0:1, :]),
                    deps=[t_bm, t_lb], inc="ev")
                t_ba_prev = t_ba
                t_fold[("b", wi)] = t_ba
                for dc in range(8):
                    t_fold[(wi, dc)] = ph.op(
                        "vector", lambda e, wsb=wsb, dc=dc: e.tensor_scalar_mul(
                            wsb[:, dc * 1024:(dc + 1) * 1024],
                            wsb[:, dc * 1024:(dc + 1) * 1024],
                            gch[:, dc:dc + 1]),
                        deps=[t_bm], inc="ev", pre_nops=1)
            t_bias_done = t_fold[("b", 2)]

            def vones(e):
                for t in range(16):
                    inst = e.memset(
                        V_sb[:, t * 1088:(t + 1) * 1088].rearrange(
                            "p (h c) -> p h c", h=16)[:, :, 64:65], 1.0)
                return inst
            t_vones = ph.op("gpsimd", vones, inc="gp")

            t_load = [None] * NT
            t_red = [None] * NT
            t_sq = [None] * NT
            t_xn = [None] * NT
            t_tp = [None] * NT
            t_ev = [None] * NT
            t_statsdone = [None] * 4
            t_evac_bank = [None, None]
            t_qk = {}
            t_qke = {}
            t_v = {}
            t_ve = {}

            def emit_stats_a2(g4):
                for ti in range(4):
                    tt = g4 * 4 + ti
                    slot = tt % 6
                    t_load[tt] = ph.op("sync", lambda e, tt=tt, slot=slot: e.dma_start(
                        xst2[:, slot * 1024:(slot + 1) * 1024],
                        x1s.ap()[tt * 128:(tt + 1) * 128, :]),
                        deps=([t_xn[tt - 6],
                               None if ext_stats else t_sq[tt - 6]]
                              if tt >= 6 else []),
                        inc=ph.ld(f"x{slot}"), n=16)
                    if ext_stats:
                        continue
                    t_red[tt] = ph.op("vector", lambda e, tt=tt, slot=slot: e.reduce_sum(
                        sm_s[:, tt:tt + 1], xst2[:, slot * 1024:(slot + 1) * 1024],
                        axis=AX.X), deps=[t_load[tt]], inc="ev")
                    t_sq[tt] = ph.op("scalar", lambda e, tt=tt, slot=slot: e.activation(
                        sqscr2[:], xst2[:, slot * 1024:(slot + 1) * 1024], AF.Square,
                        accum_out=sm_q[:, tt:tt + 1]),
                        deps=[t_load[tt], t_sq[tt - 1] if tt >= 1 else None],
                        inc="sc")
                    if tt % 4 == 3:
                        b_ = tt // 4
                        t_statsdone[b_] = stats_chain(
                            ph, b_, t_red, t_sq,
                            t_statsdone[b_ - 1] if b_ >= 1 else None)

            if ext_stats:
                for b_ in range(4):
                    t_statsdone[b_] = stats_chain(
                        ph, b_, None, None,
                        t_statsdone[b_ - 1] if b_ >= 1 else None,
                        bufs=(sm2_s, sm2_q, sm2_m, sm2_r))
            emit_stats_a2(0)
            for g4 in range(4):
                for ti in range(4):
                    tt = g4 * 4 + ti
                    slot = tt % 6
                    mm2 = sm2_m if ext_stats else sm_m
                    rr2 = sm2_r if ext_stats else sm_r
                    t_xn[tt] = ph.op("vector", lambda e, tt=tt, slot=slot,
                                     mm2=mm2, rr2=rr2: e.tensor_scalar(
                        out=xntm2[:, (tt % 2) * 1024:(tt % 2 + 1) * 1024],
                        in0=xst2[:, slot * 1024:(slot + 1) * 1024],
                        scalar1=mm2[:, tt:tt + 1], scalar2=rr2[:, tt:tt + 1],
                        op0=ALU.subtract, op1=ALU.mult),
                        deps=[t_statsdone[tt // 4], t_load[tt],
                              t_tp[tt - 2] if tt >= 2 else None],
                        inc="ev")
                    bank = tt % 2
                    tview = PS0[:, bank * 512:(bank + 1) * 512].bitcast(BF16)

                    def tp_fn(e, tt=tt, tview=tview):
                        for dc in range(8):
                            inst = e.transpose(
                                tview[:, dc * 128:(dc + 1) * 128],
                                xntm2[:, (tt % 2) * 1024 + dc * 128:
                                      (tt % 2) * 1024 + (dc + 1) * 128],
                                ident[:])
                        return inst

                    t_tp[tt] = ph.op("tensor", tp_fn,
                                     deps=[t_xn[tt], t_evac_bank[bank],
                                           t_bias_done if tt <= 1 else None],
                                     inc="mm")
                    gp_ = (tt // 4) % 2
                    t_ev[tt] = ph.op("vector", lambda e, tview=tview, ti=ti, gp_=gp_: e.tensor_copy(
                        xnfm2[:, gp_ * 4096:(gp_ + 1) * 4096].rearrange(
                            "p (a t) -> p a t", a=8)[:, :, ti * 128:(ti + 1) * 128],
                        tview[:].rearrange("p (a t) -> p a t", a=8)),
                        deps=[t_tp[tt], t_v.get((g4 - 2, 3, 1))], inc="ev")
                    t_evac_bank[tt % 2] = t_ev[tt]

                if g4 < 3:
                    emit_stats_a2(g4 + 1)
                grp_ev = t_ev[g4 * 4 + 3]
                fold_deps = [t_fold[(wi, dc)] for wi in range(3) for dc in range(8)] \
                    if g4 == 0 else []

                for pi, (wsb, dst_sb) in enumerate(((Wqs, Q_sb), (Wks, K_sb))):
                    for oc in range(8):
                        par = oc % 2
                        qview = PS0[:, 1024 + par * 512: 1536 + par * 512]

                        gp_ = g4 % 2

                        def qk_fn(e, wsb=wsb, oc=oc, pi=pi, qview=qview, gp_=gp_):
                            for dc in range(8):
                                e.matmul(qview,
                                         wsb[:, dc * 1024 + oc * 128:
                                             dc * 1024 + (oc + 1) * 128],
                                         xnfm2[:, gp_ * 4096 + dc * 512:
                                               gp_ * 4096 + (dc + 1) * 512],
                                         start=(dc == 0), stop=False)
                            inst = e.matmul(qview,
                                            bb16[0:1, pi * 1024 + oc * 128:
                                                 pi * 1024 + (oc + 1) * 128],
                                            ones_r[0:1, 0:512], start=False, stop=True)
                            return inst

                        if pi == 1 and oc < 2:
                            bank_dep = t_qke.get((g4, 0, 6 + par))
                        else:
                            bank_dep = t_qke.get((g4, pi, oc - 2))
                        if pi == 0 and oc < 2:
                            bank_dep = t_qke.get((g4 - 1, 1, 6 + par))
                        t_qk[(g4, pi, oc)] = ph.op(
                            "tensor", qk_fn,
                            deps=[grp_ev, bank_dep] + fold_deps,
                            inc="mm")
                        fold_deps = []
                        t_qke[(g4, pi, oc)] = ph.op(
                            "vector", lambda e, d=dst_sb, oc=oc, g4=g4, qview=qview: e.tensor_copy(
                                d[:, oc * 2048 + g4 * 512: oc * 2048 + (g4 + 1) * 512],
                                qview),
                            deps=[t_qk[(g4, pi, oc)]], inc="ev")

                for ti in range(4):
                    tt = g4 * 4 + ti
                    for ov in range(2):
                        vview = PS1[:, ov * 512:(ov + 1) * 512]
                        if ti == 0:
                            vdep = t_ve.get((g4 - 1, 3, ov))
                        else:
                            vdep = t_ve.get((g4, ti - 1, ov))

                        gp_ = g4 % 2

                        def v_fn(e, ti=ti, ov=ov, vview=vview, gp_=gp_):
                            for dc in range(8):
                                e.matmul(vview,
                                         xnfm2[:, gp_ * 4096 + dc * 512 + ti * 128:
                                               gp_ * 4096 + dc * 512 + (ti + 1) * 128],
                                         Wvs[:, dc * 1024 + ov * 512:
                                             dc * 1024 + (ov + 1) * 512],
                                         start=(dc == 0), stop=False)
                            inst = e.matmul(vview, ones_r[0:1, 0:128],
                                            bb16[0:1, 2048 + ov * 512:
                                                 2048 + (ov + 1) * 512],
                                            start=False, stop=True)
                            return inst

                        t_v[(g4, ti, ov)] = ph.op(
                            "tensor", v_fn, deps=[grp_ev, vdep], inc="mm")
                        t_ve[(g4, ti, ov)] = ph.op(
                            "vector", lambda e, tt=tt, ov=ov, vview=vview: e.tensor_copy(
                                V_sb[:, tt * 1088 + ov * 8 * 68:
                                     tt * 1088 + (ov * 8 + 8) * 68].rearrange(
                                         "p (h c) -> p h c", h=8)[:, :, 0:64],
                                vview.rearrange("p (h c) -> p h c", h=8)),
                            deps=[t_v[(g4, ti, ov)], t_vones], inc="ev")

            flush(ph)
            ph.run()

    def emit_norms(ph, K_sb, Q_sb, nsq):
        t_red = {}
        for pi, src_sb in enumerate((Q_sb, K_sb)):
            t_sq = {}
            t_mm = {}
            for dc in range(8):
                sl = dc % 2
                t_sq[dc] = ph.op("vector", lambda e, s=src_sb, dc=dc, sl=sl: e.tensor_mul(
                    nsq[:, sl * 2048:(sl + 1) * 2048],
                    s[:, dc * 2048:(dc + 1) * 2048],
                    s[:, dc * 2048:(dc + 1) * 2048]),
                    deps=[t_mm.get(dc - 2), t_red.get(pi - 1),
                          t_sq.get(dc - 1)], inc="ev")

                def nmm(e, dc=dc, sl=sl):
                    for j in range(4):
                        inst = e.matmul(
                            PS0[0:16, j * 512:(j + 1) * 512],
                            selp[:, dc * 16:(dc + 1) * 16],
                            nsq[:, sl * 2048 + j * 512: sl * 2048 + (j + 1) * 512],
                            start=(dc == 0), stop=(dc == 7))
                    return inst

                t_mm[dc] = ph.op("tensor", nmm,
                                 deps=[t_sq[dc], t_red.get(pi - 1)], inc="mm")
            t_red[pi] = ph.op("vector", lambda e, pi=pi: e.reduce_max(
                n16[:, pi:pi + 1],
                PS0[0:16, :].rearrange("p (a t) -> p a t", a=4), axis=AX.XY),
                deps=[t_mm[7]], inc="ev")
        t_p = ph.op("vector", lambda e: e.tensor_mul(
            n16[:, 2:3], n16[:, 0:1], n16[:, 1:2]),
            deps=[t_red[1]], inc="ev", pre_drain=True)
        t_s = ph.op("scalar", lambda e: e.activation(
            n16[:, 3:4], n16[:, 2:3], AF.Sqrt), deps=[t_p], inc="sc")
        t_c = ph.op("vector", lambda e: e.tensor_scalar_mul(
            n16[:, 4:5], n16[:, 3:4], -1.01 * 0.125),
            deps=[t_s], inc="ev")
        t_t = ph.op("tensor", lambda e: e.matmul(
            PS1[0:1, 0:16], n16[0:16, 4:5], id16[:], start=True, stop=True),
            deps=[t_c], inc="mm")
        t_cr = ph.op("vector", lambda e: e.tensor_copy(
            crow[0:1, :], PS1[0:1, 0:16]), deps=[t_t], inc="ev")
        t_pm = ph.op("vector", lambda e: e.tensor_tensor(
            crow2[0:1, :],
            crow[0:1, :].rearrange("a (h two) -> a h two", two=2)[:, :, 0],
            crow[0:1, :].rearrange("a (h two) -> a h two", two=2)[:, :, 1],
            op=ALU.min), deps=[t_cr], inc="ev", pre_drain=True)
        t_bc = ph.op("tensor", lambda e: e.matmul(
            PS1[:, 512:520], ones_r[0:1, 0:128], crow2[0:1, :],
            start=True, stop=True), deps=[t_pm], inc="mm")
        return ph.op("vector", lambda e: e.tensor_copy(cb8[:], PS1[:, 512:520]),
                     deps=[t_bc], inc="ev")

    def attn_phase(seq, K_sb, Q_sb, V_sb):
        nm = f"att{seq}"
        ph = Phase(P, nm)
        with (
            nc.sbuf_tensor(nm + "_nsq", [128, 2 * 2048], BF16) as nsq,
            nc.sbuf_tensor(nm + "_eS", [128, 2 * 16 * 1024], BF16) as expS,
            nc.sbuf_tensor(nm + "_x1q", [128, 4 * 1024], F32) as x1q,
            nc.sbuf_tensor(nm + "_x2b", [128, 4 * 1024], BF16) as x2b,
            nc.sbuf_tensor(nm + "_x2o", [128, 2 * 1024], F32) as x2o,
            nc.sbuf_tensor(nm + "_lb", [128, 16], F32) as lbuf,
        ):
            t_cb = emit_norms(ph, K_sb, Q_sb, nsq)
            t_x2add = {}
            t_store = {}
            t_exp = {}
            t_pvc = {}
            t_yev = {}
            t_x1 = {}
            pv_war = {0: t_cb, 1: t_cb}
            exp_last = {0: t_cb, 1: t_cb}
            pend = [None]

            def emit_pv_chunk(qg, hp, c):
                buf = (hp % 2) * 16384
                qi = c // 2
                half = c % 2
                qtp = qi // 2
                par = qi % 2
                h = hp * 2 + par

                def pv_fn(e, h=h, par=par, qtp=qtp, half=half, buf=buf):
                    for kt in range(half * 8, half * 8 + 8):
                        for q2 in range(2):
                            qt = qtp * 2 + q2
                            inst = e.matmul(
                                PS1[:, (par * 2 + q2) * 512:
                                     (par * 2 + q2) * 512 + 65],
                                expS[:, buf + kt * 1024 + par * 512 + qt * 128:
                                     buf + kt * 1024 + par * 512 + (qt + 1) * 128],
                                V_sb[:, kt * 1088 + h * 68: kt * 1088 + h * 68 + 65],
                                start=(kt == 0), stop=(kt == 15))
                    return inst

                deps = [t_exp[(qg, hp, 15)]]
                if half == 0 and par == 0:
                    if qtp == 1:
                        deps.append(t_yev.get((qg, hp, 0)))
                    else:
                        pq = (qg, hp - 1) if hp >= 1 else (qg - 1, 7)
                        deps.append(t_yev.get((pq[0], pq[1], 1)))
                        deps.append(t_cb)
                tok = ph.op("tensor", pv_fn, deps=deps, inc="mm")
                t_pvc[(qg, hp, c)] = tok
                pv_war[hp % 2] = tok
                if c in (3, 7):
                    qtp = (c - 3) // 4
                    pvdeps = [t_pvc[(qg, hp, qtp * 4 + k)] for k in range(4)]
                    t_l = ph.op("vector", lambda e, qtp=qtp: e.tensor_scalar(
                        out=lbuf[:, qtp * 4: qtp * 4 + 4][:, :, None],
                        in0=PS1[:, :].rearrange("p (a t) -> p a t", a=4)[:, :, 64:65],
                        scalar1=1e-30, scalar2=None, op0=ALU.max),
                        deps=pvdeps, inc="ev")
                    t_r = ph.op("vector", lambda e, qtp=qtp: e.reciprocal(
                        lbuf[:, 8 + qtp * 4: 8 + qtp * 4 + 4],
                        lbuf[:, qtp * 4: qtp * 4 + 4]),
                        deps=[t_l], inc="ev", pre_drain=True)
                    t_yev[(qg, hp, qtp)] = ph.op(
                        "vector", lambda e, hp=hp, qtp=qtp: e.tensor_mul(
                            x2b[:].rearrange("p (q2 hh t) -> p q2 hh t", q2=4, hh=16)[
                                :, 2 * qtp:2 * qtp + 2, 2 * hp:2 * hp + 2, :]
                            .transpose([0, 2, 1, 3]),
                            PS1[:, :].rearrange("p (par q2 t) -> p par q2 t",
                                                par=2, q2=2)[:, :, :, 0:64],
                            lbuf[:, 8 + qtp * 4: 8 + qtp * 4 + 4]
                            .rearrange("p (par q2) -> p par q2", par=2)[:, :, :, None]
                            .broadcast_to([128, 2, 2, 64])),
                        deps=[t_r, t_x2add.get((qg - 1, 3))], inc="ev")

            def drain_qg(qg):
                for qt in range(4):
                    fi = qg * 4 + qt
                    t_x2add[(qg, qt)] = ph.op("vector", lambda e, qt=qt: e.tensor_add(
                        x2o[:, (qt % 2) * 1024:(qt % 2 + 1) * 1024],
                        x2b[:, qt * 1024:(qt + 1) * 1024],
                        x1q[:, qt * 1024:(qt + 1) * 1024]),
                        deps=[t_yev[(qg, 7, qt // 2)], t_x1[qg],
                              t_store.get(fi - 2)], inc="ev")
                    t_store[fi] = ph.op("gpsimd", lambda e, fi=fi, qt=qt: e.dma_start(
                        x2s.ap()[fi * 128:(fi + 1) * 128, :],
                        x2o[:, (qt % 2) * 1024:(qt % 2 + 1) * 1024]),
                        deps=[t_x2add[(qg, qt)]], inc=ph.st(f"o{qt % 2}"), n=16)

            t_x1[0] = ph.op("sync", lambda e: e.dma_start(
                x1q[:].rearrange("p (a t) -> p a t", a=4),
                x1s.ap().rearrange("(a p) t -> p a t", p=128)[:, 0:4, :]),
                inc=ph.ld("xq"), n=16)
            for qg in range(4):
                for hp in range(8):
                    buf = (hp % 2) * 16384
                    for r in range(16):
                        pp = (r % 2) * 1024

                        def st_fn(e, hp=hp, r=r, qg=qg, pp=pp):
                            for par in range(2):
                                inst = e.matmul(
                                    PS0[:, pp + par * 512: pp + (par + 1) * 512],
                                    K_sb[64 * par:64 * par + 64,
                                         hp * 2048 + r * 128:
                                         hp * 2048 + (r + 1) * 128],
                                    Q_sb[64 * par:64 * par + 64,
                                         hp * 2048 + qg * 512:
                                         hp * 2048 + (qg + 1) * 512],
                                    start=True, stop=True)
                            return inst

                        t_mm = ph.op("tensor", st_fn, deps=[exp_last[r % 2]],
                                     inc="mm")
                        t_exp[(qg, hp, r)] = ph.op(
                            "scalar", lambda e, hp=hp, r=r, buf=buf, pp=pp: e.activation(
                                expS[:, buf + r * 1024: buf + (r + 1) * 1024],
                                PS0[:, pp:pp + 1024], AF.Exp,
                                bias=cb8[:, hp:hp + 1], scale=0.125),
                            deps=[t_mm, pv_war[hp % 2], t_cb], inc="sc")
                        exp_last[r % 2] = t_exp[(qg, hp, r)]
                        if pend[0] is not None and r % 2 == 1:
                            emit_pv_chunk(pend[0][0], pend[0][1], r // 2)
                    pend[0] = (qg, hp)
                    if hp == 0 and qg > 0:
                        drain_qg(qg - 1)
                        t_x1[qg] = ph.op("sync", lambda e, qg=qg: e.dma_start(
                            x1q[:].rearrange("p (a t) -> p a t", a=4),
                            x1s.ap().rearrange("(a p) t -> p a t", p=128)[
                                :, qg * 4: qg * 4 + 4, :]),
                            deps=[t_x2add[(qg - 1, 3)]], inc=ph.ld("xq"), n=16)
            for c in range(8):
                emit_pv_chunk(3, 7, c)
            drain_qg(3)
            flush(ph)
            ph.run()

    if "prep" in stages:
        prep()
        php = Phase(P, "prep_w")
        with (
            nc.sbuf_tensor("pp_ws", [128, 4 * HID], I32) as pws,
            nc.sbuf_tensor("pp_wb", [128, 4 * HID], BF16) as pwb,
            nc.sbuf_tensor("pp_wo", [128, 4 * HID], BF16) as pwo,
        ):
            conv = {}
            evac = {}
            stor = {}
            tps = {}
            ci = 0
            bank = [0]
            for (wsrc, wdstT, nch, cw) in ((wg_d, wgT, 24, D), (wu_d, wuT, 24, D),
                                           (wd_d, wdT, 8, HID)):
                ntile = cw // 128
                for ch in range(nch):
                    sl = ci % 4
                    t_l = php.op("sync", lambda e, s=wsrc, c=ch, sl=sl, cw=cw: e.dma_start(
                        pws[:, sl * HID: sl * HID + cw],
                        s.ap()[c * 128:(c + 1) * 128, :]),
                        deps=[conv.get(ci - 4)], inc=php.ld(f"w{sl}"), n=16)
                    conv[ci] = php.op("vector", lambda e, sl=sl, cw=cw: e.tensor_scalar_mul(
                        pwb[:, sl * HID: sl * HID + cw],
                        pws[:, sl * HID: sl * HID + cw], QS),
                        deps=[t_l, tps.get(ci - 4)], inc="ev")
                    nround = (ntile + 7) // 8
                    for rd in range(nround):
                        t0, t1 = rd * 8, min(ntile, rd * 8 + 8)
                        bk = bank[0] % 2
                        bank[0] += 1
                        tview = PS0[:, bk * 512:(bk + 1) * 512].bitcast(BF16)

                        def tp_fn(e, sl=sl, t0=t0, t1=t1, tview=tview):
                            for dc in range(t0, t1):
                                inst = e.transpose(
                                    tview[:, (dc - t0) * 128:(dc - t0 + 1) * 128],
                                    pwb[:, sl * HID + dc * 128: sl * HID + (dc + 1) * 128],
                                    ident[:])
                            return inst

                        t_tp = php.op("tensor", tp_fn,
                                      deps=[conv[ci], evac.get(bank[0] - 3)],
                                      inc="mm")
                        tps[ci] = t_tp
                        t_e = php.op("vector", lambda e, sl=sl, t0=t0, t1=t1, tview=tview: e.tensor_copy(
                            pwo[:, sl * HID + t0 * 128: sl * HID + t1 * 128],
                            tview[:, 0:(t1 - t0) * 128]),
                            deps=[t_tp, stor.get(ci - 4)], inc="ev")
                        evac[(ci, rd)] = t_e
                        evac[bank[0] - 1] = t_e
                    if cw == D:
                        def st_ap(c):
                            return bass.AP(
                                wdstT, (c // 2) * 262144 + (c % 2) * 128,
                                [[256, 128], [32768, 8], [1, 128]])
                        out_ap = st_ap(ch)
                    else:
                        out_ap = bass.AP(
                            wdT, ch * 128, [[1024, 128], [131072, 24], [1, 128]])
                    edeps = [evac[(ci, rd)] for rd in range(nround)]
                    stor[ci] = php.op("gpsimd", lambda e, sl=sl, cw=cw, out_ap=out_ap: e.dma_start(
                        out_ap,
                        pwo[:, sl * HID: sl * HID + cw].rearrange(
                            "p (a h) -> p a h", a=cw // 128)),
                        deps=edeps, inc=php.st(f"pb{sl}"), n=16)
                    ci += 1
            flush(php)
            php.run()
    for s in range(NSEQ):
        a2ext = "a1" in stages
        if "a1" in stages:
            ffn_phase(f"a1s{s}", x_d.ap()[s], x1s.ap(), next_stats=a2ext)
            if debug and s == 0:
                phd = Phase(P, "dbgsm")
                for i, buf in enumerate((sm_s, sm_q, sm_m, sm_r)):
                    phd.op("gpsimd", lambda e, i=i, buf=buf: e.dma_start(
                        dbg_sm.ap()[:, i * 16:(i + 1) * 16], buf[:]),
                        inc=phd.st(f"d{i}"), n=16)
                flush(phd)
                phd.run()
        if {"a2", "norm", "b"} & set(stages):
            with (
                nc.sbuf_tensor(f"K{s}", [128, 8 * 2048], BF16) as K_sb,
                nc.sbuf_tensor(f"Q{s}", [128, 8 * 2048], BF16) as Q_sb,
                nc.sbuf_tensor(f"V{s}", [128, 16 * 1088], BF16) as V_sb,
            ):
                if "a2" in stages:
                    qkv_phase(s, K_sb, Q_sb, V_sb, ext_stats=a2ext)
                if "b" in stages:
                    attn_phase(s, K_sb, Q_sb, V_sb)
        if "c" in stages:
            ffn_phase(f"cs{s}", x2s.ap(), out_d.ap()[s])
        if "one_seq" in stages:
            break

    ph = Phase(P, "fin")
    for i in range(8):
        ph.op("gpsimd", lambda e, l=f"st{i}": e.wait_ge(sems[l], P.C[l]))
    ph.run()

    es.close()
    return nc


_NC_CACHE = [None]


def kernel(**inputs):
    x = np.ascontiguousarray(np.asarray(inputs["L0_attn_x"], dtype=np.float32))
    full = {k: np.ascontiguousarray(np.asarray(v)) for k, v in inputs.items()
            if k != "L0_attn_x"}
    if _NC_CACHE[0] is None:
        _NC_CACHE[0] = build_program()
    nc = _NC_CACHE[0]
    in_maps = []
    for c in range(8):
        m = dict(full)
        m["L0_attn_x"] = x[c * NSEQ:(c + 1) * NSEQ]
        in_maps.append(m)
    res = run_bass_kernel_spmd(nc, in_maps, core_ids=list(range(8)))
    out = np.concatenate([np.asarray(r["out"]) for r in res.results], axis=0)
    return out.astype(np.float32)
